# version 07
# baseline (speedup 1.0000x reference)
import numpy as np
import concourse.bass as bass
import concourse.mybir as mybir
from concourse.bass_utils import run_bass_kernel_spmd
from contextlib import ExitStack

F32 = mybir.dt.float32
BF16 = mybir.dt.bfloat16
I32 = mybir.dt.int32
AF = mybir.ActivationFunctionType
ALU = mybir.AluOpType
AX = mybir.AxisListType

T, D = 2048, 1024
NT = T // 128
NC = D // 128
NEG = -1e30


class Prog:
    def __init__(self):
        self.steps = []
        self.cum = {"T": 0}

    def em(self, eng, fn, inc=1, sem="T", waits=None):
        self.steps.append((eng, fn, inc, sem, waits))
        self.cum[sem] = self.cum.get(sem, 0) + inc

    def dma(self, eng, out, in_, sem="T", waits=None):
        self.em(eng, lambda e, o=out, i=in_: e.dma_start(out=o, in_=i), inc=16,
                sem=sem, waits=waits)

    def mark(self, sem="T"):
        return self.cum.get(sem, 0)

    def finalize(self, nc, block, sems):
        waits, cumt = [], 0
        for (_, _, inc, sk, _) in self.steps:
            waits.append(cumt)
            if sk == "T":
                cumt += inc
        totals = dict(self.cum)
        SKIP_OK = ("tensor",)
        def group(step):
            en, _, inc, sk, _ = step
            return en + ("_dma" if inc == 16 else "") + sk
        prev_idx = {}
        need_wait = []
        for i, st in enumerate(self.steps):
            g = group(st)
            pi = prev_idx.get(g)
            if st[4] is not None:
                need_wait.append(True)
            elif pi is None or g not in SKIP_OK:
                need_wait.append(waits[i] > 0)
            else:
                gap = any(group(self.steps[j]) != g for j in range(pi + 1, i))
                need_wait.append(gap)
            prev_idx[g] = i
        for eng_name in ["sync", "scalar", "vector", "tensor", "gpsimd"]:
            def body(eng, eng_name=eng_name):
                for (en, fn, inc, sk, wv), w, nw in zip(self.steps, waits, need_wait):
                    if en != eng_name:
                        continue
                    if wv is not None:
                        for k, v in wv.items():
                            if v > 0:
                                eng.wait_ge(sems[k], v)
                    elif nw and w > 0:
                        eng.wait_ge(sems["T"], w)
                    fn(eng).then_inc(sems[sk], inc)
                for k, v in totals.items():
                    if v > 0:
                        eng.wait_ge(sems[k], v)
            getattr(block, eng_name)(body)
        return totals


_CONSTS = None


def host_consts():
    global _CONSTS
    if _CONSTS is not None:
        return _CONSTS
    ident = np.eye(128, dtype=np.float32)
    jj = np.arange(128)[None, :]
    pp = np.arange(128)[:, None]
    m0t = np.where(jj >= pp, 0.0, NEG).astype(np.float32)
    tri01 = (jj >= pp).astype(np.float32)
    t = np.arange(T, dtype=np.float64)
    inv = 1.0 / (10000.0 ** (np.arange(0, 64, 2, dtype=np.float64) / 64))
    fr = np.outer(t, inv)
    c = np.cos(fr); s = np.sin(fr)
    r0c = np.tile(np.concatenate([c, c], 1), (1, 16)).astype(np.float32)
    r0s = np.tile(np.concatenate([s, -s], 1), (1, 16)).astype(np.float32)
    inv2 = 10000.0 ** (-2.0 * np.arange(32, dtype=np.float64) / 64)
    ang = np.outer(t, inv2)
    c2 = np.repeat(np.cos(ang), 2, axis=1); s2 = np.repeat(np.sin(ang), 2, axis=1)
    r1c = np.tile(c2, (1, 16)).astype(np.float32)
    r1s = np.tile(s2, (1, 16)).astype(np.float32)
    _CONSTS = dict(ident=ident, m0t=m0t, tri01=tri01, r0c=r0c, r0s=r0s, r1c=r1c, r1s=r1s)
    return _CONSTS


class Ctx:
    def __init__(self, nc, p, es):
        self.nc, self.p, self.es = nc, p, es
        self.cur = es
        self.prefix = ""
        self._inps = {}
        self._internals = {}
        self._nsb = 0

    def sb(self, name, shape, dtype):
        self._nsb += 1
        return self.cur.enter_context(self.nc.sbuf_tensor(f"{name}_{self._nsb}", shape, dtype))

    def mk(self, stack):
        def alloc(n, sh, dt):
            self._nsb += 1
            return stack.enter_context(self.nc.sbuf_tensor(f"{n}_{self._nsb}", sh, dt))
        return alloc

    def ps(self, name, shape, dtype=F32):
        return self.es.enter_context(self.nc.psum_tensor(name, shape, dtype))

    def inp(self, name, shape, dtype=F32):
        nm = self.prefix + name
        if nm not in self._inps:
            self._inps[nm] = self.nc.dram_tensor(nm, shape, dtype, kind="ExternalInput").ap()
        return self._inps[nm]

    def internal(self, name, shape, dtype=F32):
        nm = self.prefix + name
        if nm not in self._internals:
            self._internals[nm] = self.nc.dram_tensor(nm, shape, dtype, kind="Internal").ap()
        return self._internals[nm]

    def setup_common(self):
        p = self.p
        self.identF = self.inp("identF", [128, 128])
        self.m0tF = self.inp("m0tF", [128, 128])
        self.triF = self.inp("triF", [128, 128])
        self.sIdent = self.sb("c_ident", [128, 128], F32)
        self.sIdentB = self.sb("c_identb", [128, 128], BF16)
        self.sM0 = self.sb("c_m0", [128, 128], F32)
        self.sTri = self.sb("c_tri", [128, 128], F32)
        self.sOnes = self.sb("c_ones", [1, 128], F32)
        p.dma("sync", self.sIdent[:, :], self.identF)
        p.dma("sync", self.sM0[:, :], self.m0tF)
        p.dma("sync", self.sTri[:, :], self.triF)
        p.em("vector", lambda e: e.tensor_copy(out=self.sIdentB[:, :], in_=self.sIdent[:, :]))
        p.em("vector", lambda e: e.memset(self.sOnes[:, :], 1.0))
        self.scA = self.sb("scrapA", [128, 1024], F32)
        self.scB = self.sb("scrapB", [128, 1024], F32)
        self.bB = self.sb("scrapBF", [128, 1024], BF16)
        self.xtile = self.sb("xtile", [128, 1024], F32)
        self.st = self.sb("stats", [128, 64], F32)
        self.psA = self.ps("psA", [128, 512])
        self.psB = self.ps("psB", [128, 512])
        self.psO = self.ps("psO", [128, 512])
        self.psT = self.ps("psT", [128, 128])
        self.psTB = self.ps("psTB", [128, 128], BF16)

    def tr(self, src_ap, dst_ap, bf16=True, np_=128, nf=128):
        p = self.p
        ident = self.sIdentB if bf16 else self.sIdent
        pst = self.psTB if bf16 else self.psT
        p.em("tensor", lambda e: e.transpose(pst[0:nf, 0:np_], src_ap, ident[0:np_, 0:np_]))
        p.em("scalar", lambda e: e.copy(out=dst_ap, in_=pst[0:nf, 0:np_]))

    def bcast_row(self, row_ap, dst_ap, nparts, ncols):
        p = self.p
        p.em("tensor", lambda e: e.matmul(self.psT[0:nparts, 0:ncols], self.sOnes[0:1, 0:nparts],
                                          row_ap, start=True, stop=True))
        p.em("scalar", lambda e: e.copy(out=dst_ap, in_=self.psT[0:nparts, 0:ncols]))

    def norm_tile(self, x_ap, z32_ap, zb_ap, rms, eps):
        p, st, scA, scB = self.p, self.st, self.scA, self.scB
        n = 1024.0
        src = x_ap
        if not rms:
            p.em("vector", lambda e: e.reduce_sum(st[:, 0:1], x_ap, axis=AX.X))
            p.em("vector", lambda e: e.tensor_scalar(out=st[:, 0:1], in0=st[:, 0:1],
                                                     scalar1=-1.0 / n, scalar2=None, op0=ALU.mult))
            p.em("vector", lambda e: e.tensor_scalar_add(scA[:, :], x_ap, st[:, 0:1]))
            src = scA[:, :]
        p.em("scalar", lambda e, src=src: e.activation(scB[:, :], src, AF.Square,
                                                       accum_out=st[:, 1:2]))
        p.em("vector", lambda e: e.tensor_scalar(out=st[:, 1:2], in0=st[:, 1:2],
                                                 scalar1=1.0 / n, scalar2=eps, op0=ALU.mult, op1=ALU.add))
        p.em("scalar", lambda e: e.sqrt(st[:, 2:3], st[:, 1:2]))
        p.em("vector", lambda e: e.reciprocal(st[:, 3:4], st[:, 2:3]))
        if z32_ap is not None:
            p.em("vector", lambda e, src=src: e.tensor_scalar_mul(z32_ap, src, st[:, 3:4]))
        if zb_ap is not None:
            p.em("vector", lambda e, src=src: e.tensor_scalar_mul(zb_ap, src, st[:, 3:4]))


def load_w_bf16(cx, hbm_ap, slab, rows, cols):
    p = cx.p
    for j in range(rows // 128):
        p.dma("sync", cx.scA[:, 0:cols], hbm_ap[j * 128:(j + 1) * 128, :])
        p.em("vector", lambda e, j=j: e.tensor_copy(out=slab[:, j * cols:(j + 1) * cols],
                                                    in_=cx.scA[:, 0:cols]))


def emit_ln_zT(cx, xin, zT, rms=False, z32slab=None, zbslab=None):
    p = cx.p
    for i in range(NT):
        p.dma("gpsimd", cx.xtile[:, :], xin[i * 128:(i + 1) * 128, :])
        z32 = z32slab[:, i * 1024:(i + 1) * 1024] if z32slab is not None else None
        zb = zbslab[:, i * 1024:(i + 1) * 1024] if zbslab is not None else cx.bB[:, :]
        cx.norm_tile(cx.xtile[:, :], z32, zb, rms, 1e-5)
        if zT is not None:
            for j in range(NC):
                cx.tr(zb[:, j * 128:(j + 1) * 128], zT[j][:, i * 128:(i + 1) * 128], bf16=True)


def emit_l0attn(cx):
    nc, p, es = cx.nc, cx.p, cx.cur
    if True:
        xin = cx.xin
        wqT = cx.inp("wqT", [D, D])
        wkT = cx.inp("wkT", [D, D])
        wvT = cx.inp("wvT", [D, D])
        r1c = cx.inp("r1c", [T, D])
        r1s = cx.inp("r1s", [T, D])
        xout = cx.xout

        qT = [cx.sb(f"qT{j}", [128, T], BF16) for j in range(NC)]
        kT = [cx.sb(f"kT{j}", [128, T], BF16) for j in range(NC)]
        vext = cx.sb("vext", [128, NT * 16 * 65], BF16)
        qmax = cx.sb("qmax", [128, 16], F32)
        kmax = cx.sb("kmax", [128, 16], F32)
        es1 = es.enter_context(ExitStack())
        sb1 = cx.mk(es1)
        zT = [sb1(f"zT{j}", [128, T], BF16) for j in range(NC)]
        emit_ln_zT(cx, xin, zT, rms=False)
        wslab = sb1("wslab", [128, NC * 1024], BF16)
        ctile = sb1("ctile", [128, 1024], F32)
        stile = sb1("stile", [128, 1024], F32)
        p.em("vector", lambda e: e.memset(qmax[:, :], 0.0))
        p.em("vector", lambda e: e.memset(kmax[:, :], 0.0))

        def proj_rope(outT, maxacc):
            for i in range(NT):
                p.dma("gpsimd", ctile[:, :], r1c[i * 128:(i + 1) * 128, :])
                p.dma("gpsimd", stile[:, :], r1s[i * 128:(i + 1) * 128, :])
                for ch in range(2):
                    psp = cx.psA if ch == 0 else cx.psB
                    for j in range(NC):
                        p.em("tensor", lambda e, i=i, j=j, ch=ch, psp=psp:
                             e.matmul(psp[:, :], zT[j][:, i * 128:(i + 1) * 128],
                                      wslab[:, j * 1024 + ch * 512: j * 1024 + (ch + 1) * 512],
                                      start=(j == 0), stop=(j == NC - 1)))
                    p.em("scalar", lambda e, ch=ch, psp=psp:
                         e.copy(out=cx.scA[:, ch * 512:(ch + 1) * 512], in_=psp[:, :]))
                q2 = cx.scA[:, :].rearrange("p (a two) -> p a two", two=2)
                r2 = cx.scB[:, :].rearrange("p (a two) -> p a two", two=2)
                p.em("vector", lambda e: e.tensor_scalar_mul(r2[:, :, 0], q2[:, :, 1], -1.0))
                p.em("vector", lambda e: e.tensor_copy(out=r2[:, :, 1], in_=q2[:, :, 0]))
                p.em("vector", lambda e: e.tensor_mul(cx.scA[:, :], cx.scA[:, :], ctile[:, :]))
                p.em("vector", lambda e: e.tensor_mul(cx.scB[:, :], cx.scB[:, :], stile[:, :]))
                p.em("vector", lambda e: e.tensor_add(cx.scA[:, :], cx.scA[:, :], cx.scB[:, :]))
                p.em("scalar", lambda e: e.activation(cx.scB[:, :], cx.scA[:, :], AF.Square))
                p.em("vector", lambda e: e.reduce_sum(
                    cx.st[:, 8:24], cx.scB[:, :].rearrange("p (h d) -> p h d", h=16), axis=AX.X))
                p.em("vector", lambda e, m=maxacc: e.tensor_max(m[:, :], m[:, :], cx.st[:, 8:24]))
                p.em("vector", lambda e: e.tensor_copy(out=cx.bB[:, :], in_=cx.scA[:, :]))
                for j in range(NC):
                    cx.tr(cx.bB[:, j * 128:(j + 1) * 128], outT[j][:, i * 128:(i + 1) * 128])

        load_w_bf16(cx, wqT, wslab, D, 1024)
        proj_rope(qT, qmax)
        load_w_bf16(cx, wkT, wslab, D, 1024)
        proj_rope(kT, kmax)
        load_w_bf16(cx, wvT, wslab, D, 1024)
        for i in range(NT):
            base = i * 16 * 65
            for ch in range(2):
                psp = cx.psA if ch == 0 else cx.psB
                for j in range(NC):
                    p.em("tensor", lambda e, i=i, j=j, psp=psp, ch=ch:
                         e.matmul(psp[:, :], zT[j][:, i * 128:(i + 1) * 128],
                                  wslab[:, j * 1024 + ch * 512: j * 1024 + (ch + 1) * 512],
                                  start=(j == 0), stop=(j == NC - 1)))
                dst = vext[:, base + ch * 8 * 65: base + (ch + 1) * 8 * 65].rearrange(
                    "p (h w) -> p h w", w=65)
                p.em("vector", lambda e, psp=psp, dst=dst:
                     e.tensor_copy(out=dst[:, :, 0:64], in_=psp[:, :].rearrange("p (h d) -> p h d", d=64)))
            oc = vext[:, base: base + 16 * 65].rearrange("p (h w) -> p h w", w=65)[:, :, 64]
            p.em("vector", lambda e, oc=oc: e.memset(oc, 1.0))

        es1.close()
        mrow = cx.sb("mrow", [1, 16], F32)
        mneg = cx.sb("mneg", [128, 16], F32)
        tmax = cx.sb("tmax", [16, 128], F32)
        for col, acc in ((33, qmax), (34, kmax)):
            cx.tr(acc[:, :], tmax[0:16, 0:128], bf16=False, np_=128, nf=16)
            p.em("vector", lambda e: e.reduce_max(cx.st[0:16, 32:33], tmax[0:16, 0:128], axis=AX.X))
            p.em("scalar", lambda e, col=col: e.sqrt(cx.st[0:16, col:col + 1], cx.st[0:16, 32:33]))
        p.em("vector", lambda e: e.tensor_mul(cx.st[0:16, 35:36], cx.st[0:16, 33:34], cx.st[0:16, 34:35]))
        p.em("vector", lambda e: e.tensor_scalar_mul(cx.st[0:16, 35:36], cx.st[0:16, 35:36], -0.125))
        cx.tr(cx.st[0:16, 35:36], mrow[0:1, 0:16], bf16=False, np_=16, nf=1)
        cx.bcast_row(mrow[0:1, 0:16], mneg[:, :], 128, 16)

        attT = [cx.sb(f"attT{j}", [128, T], BF16) for j in range(NC)]
        pexp = cx.sb("pexp", [128, NT * 512], BF16)
        rrow = cx.sb("rrow", [1, 512], F32)
        rb = cx.sb("rb", [64, 512], F32)
        for h in range(16):
            jt, off = h // 2, (h % 2) * 64
            for c in range(4):
                for kt in range(NT):
                    p.em("tensor", lambda e, c=c, kt=kt, jt=jt, off=off:
                         e.matmul(cx.psA[:, :], kT[jt][off:off + 64, kt * 128:(kt + 1) * 128],
                                  qT[jt][off:off + 64, c * 512:(c + 1) * 512],
                                  start=True, stop=True))
                    p.em("scalar", lambda e, kt=kt, h=h:
                         e.activation(pexp[:, kt * 512:(kt + 1) * 512], cx.psA[:, :], AF.Exp,
                                      bias=mneg[:, h:h + 1], scale=0.125))
                for kt in range(NT):
                    p.em("tensor", lambda e, kt=kt, h=h:
                         e.matmul(cx.psO[0:65, :], vext[:, kt * 16 * 65 + h * 65: kt * 16 * 65 + (h + 1) * 65],
                                  pexp[:, kt * 512:(kt + 1) * 512],
                                  start=(kt == 0), stop=(kt == NT - 1)))
                p.em("vector", lambda e: e.reciprocal(rrow[0:1, :], cx.psO[64:65, :]))
                p.em("tensor", lambda e: e.matmul(cx.psB[0:64, :], cx.sOnes[0:1, 0:64], rrow[0:1, :],
                                                  start=True, stop=True))
                p.em("scalar", lambda e: e.copy(out=rb[:, :], in_=cx.psB[0:64, :]))
                p.em("vector", lambda e, c=c, jt=jt, off=off:
                     e.tensor_mul(attT[jt][off:off + 64, c * 512:(c + 1) * 512],
                                  cx.psO[0:64, :], rb[:, :]))
        for i in range(NT):
            p.dma("gpsimd", cx.xtile[:, :], xin[i * 128:(i + 1) * 128, :])
            for j in range(NC):
                p.em("tensor", lambda e, i=i, j=j: e.transpose(
                    cx.psTB[:, :], attT[j][:, i * 128:(i + 1) * 128], cx.sIdentB[:, :]))
                p.em("vector", lambda e, j=j: e.tensor_add(
                    cx.xtile[:, j * 128:(j + 1) * 128], cx.xtile[:, j * 128:(j + 1) * 128],
                    cx.psTB[:, :]))
            p.dma("gpsimd", xout[i * 128:(i + 1) * 128, :], cx.xtile[:, :])



def build_l0attn():
    nc = bass.Bass("TRN2", target_bir_lowering=False)
    p = Prog()
    es = ExitStack()
    with es:
        cx = Ctx(nc, p, es)
        cx.xin = cx.inp("xin", [T, D])
        cx.xout = nc.dram_tensor("xout", [T, D], F32, kind="ExternalOutput").ap()
        cx.setup_common()
        cx.cur = es.enter_context(ExitStack())
        emit_l0attn(cx)
        sem = es.enter_context(nc.semaphore("tick"))
        semP = es.enter_context(nc.semaphore("pref"))
        block = es.enter_context(nc.Block())
        p.finalize(nc, block, {"T": sem, "P": semP})
    return nc


def l0attn_inputs(full, x):
    c = host_consts()
    return {"xin": x,
            "wqT": np.ascontiguousarray(full["L0_attn_Wq"].T),
            "wkT": np.ascontiguousarray(full["L0_attn_Wk"].T),
            "wvT": np.ascontiguousarray(full["L0_attn_Wv"].T),
            "r1c": c["r1c"], "r1s": c["r1s"],
            "identF": c["ident"], "m0tF": c["m0t"], "triF": c["tri01"]}



def emit_l1ffn(cx):
    nc, p, es = cx.nc, cx.p, cx.cur
    if True:
        xin = cx.xin
        gwT = cx.inp("gwT", [D, 32])
        ebias = cx.inp("ebias", [1, 32])
        guT = cx.inp("guT", [32, D, D])
        dnT = cx.inp("dnT", [32, 512, D])
        sgT = cx.inp("sgT", [D, D])
        suT = cx.inp("suT", [D, D])
        sdT = cx.inp("sdT", [D, D])
        xout = cx.xout

        zT = [cx.sb(f"zT{j}", [128, T], BF16) for j in range(NC)]
        combine = cx.sb("combine", [128, NT * 32], F32)
        ACC = cx.sb("ACC", [128, NT * 1024], F32)
        for i in range(NT):
            p.em("vector", lambda e, i=i: e.memset(ACC[:, i * 1024:(i + 1) * 1024], 0.0))

        es1 = es.enter_context(ExitStack())
        sb1 = cx.mk(es1)
        z32t = sb1("z32t", [128, 1024], F32)
        zt32tmp = sb1("zt32tmp", [128, 1024], F32)
        gw32 = sb1("gw32", [128, NC * 32], F32)
        biasb = sb1("biasb", [128, 32], F32)
        sc = sb1("sc", [128, 32], F32)
        sfc = sb1("sfc", [128, 32], F32)
        gs = sb1("gs", [128, 8], F32)
        rkg = sb1("rkg", [128, 8], F32)
        gm = sb1("gm", [128, 8], F32)
        sm = sb1("sm", [128, 32], F32)
        msk = sb1("msk", [128, 32], F32)
        rke = sb1("rke", [128, 32], F32)
        cmp32 = sb1("cmp32", [128, 32], F32)
        for j in range(NC):
            p.dma("sync", cx.scA[:, 0:32], gwT[j * 128:(j + 1) * 128, :])
            p.em("vector", lambda e, j=j: e.tensor_copy(out=gw32[:, j * 32:(j + 1) * 32],
                                                        in_=cx.scA[:, 0:32]))
        p.dma("sync", cx.scA[0:1, 0:32], ebias)
        cx.bcast_row(cx.scA[0:1, 0:32], biasb[:, :], 128, 32)
        for i in range(NT):
            p.dma("gpsimd", cx.xtile[:, :], xin[i * 128:(i + 1) * 128, :])
            cx.norm_tile(cx.xtile[:, :], z32t[:, :], cx.bB[:, :], False, 1e-5)
            for j in range(NC):
                cx.tr(cx.bB[:, j * 128:(j + 1) * 128], zT[j][:, i * 128:(i + 1) * 128])
            for j in range(NC):
                p.em("tensor", lambda e, j=j: e.transpose(cx.psT[:, :],
                                                          z32t[:, j * 128:(j + 1) * 128],
                                                          cx.sIdent[:, :]))
                p.em("scalar", lambda e, j=j: e.copy(out=zt32tmp[:, j * 128:(j + 1) * 128],
                                                     in_=cx.psT[:, :]))
            for j in range(NC):
                p.em("tensor", lambda e, j=j: e.matmul(cx.psB[:, 0:32],
                                                       zt32tmp[:, j * 128:(j + 1) * 128],
                                                       gw32[:, j * 32:(j + 1) * 32],
                                                       start=(j == 0), stop=(j == NC - 1)))
            p.em("scalar", lambda e: e.activation(sc[:, :], cx.psB[:, 0:32], AF.Sigmoid))
            p.em("vector", lambda e: e.tensor_add(sfc[:, :], sc[:, :], biasb[:, :]))
            p.em("vector", lambda e: e.reduce_max(gs[:, :], sfc[:, :].rearrange("p (g f) -> p g f", g=8),
                                                  axis=AX.X))
            for g in range(8):
                p.em("vector", lambda e, g=g: e.tensor_single_scalar(
                    out=cx.st[:, 8:16], in_=gs[:, :], scalar=gs[:, g:g + 1], op=ALU.is_gt))
                p.em("vector", lambda e, g=g: e.reduce_sum(rkg[:, g:g + 1], cx.st[:, 8:16], axis=AX.X))
            p.em("vector", lambda e: e.tensor_single_scalar(out=gm[:, :], in_=rkg[:, :],
                                                            scalar=3.5, op=ALU.is_lt))
            for ii in range(4):
                p.em("vector", lambda e, ii=ii: e.tensor_copy(
                    out=sm[:, :].rearrange("p (g f) -> p g f", g=8)[:, :, ii], in_=gm[:, :]))
            p.em("vector", lambda e: e.tensor_mul(msk[:, :], sfc[:, :], sm[:, :]))
            for ei in range(32):
                p.em("vector", lambda e, ei=ei: e.tensor_single_scalar(
                    out=cmp32[:, :], in_=msk[:, :], scalar=msk[:, ei:ei + 1], op=ALU.is_gt))
                p.em("vector", lambda e, ei=ei: e.reduce_sum(rke[:, ei:ei + 1], cmp32[:, :], axis=AX.X))
            p.em("vector", lambda e: e.tensor_single_scalar(out=cmp32[:, :], in_=rke[:, :],
                                                            scalar=5.5, op=ALU.is_lt))
            p.em("vector", lambda e: e.tensor_mul(cmp32[:, :], cmp32[:, :], sc[:, :]))
            p.em("vector", lambda e: e.reduce_sum(cx.st[:, 4:5], cmp32[:, :], axis=AX.X))
            p.em("vector", lambda e: e.tensor_scalar_add(cx.st[:, 4:5], cx.st[:, 4:5], 1e-20))
            p.em("vector", lambda e: e.reciprocal(cx.st[:, 5:6], cx.st[:, 4:5]))
            p.em("vector", lambda e, i=i: e.tensor_scalar(
                out=combine[:, i * 32:(i + 1) * 32], in0=cmp32[:, :],
                scalar1=cx.st[:, 5:6], scalar2=2.5, op0=ALU.mult, op1=ALU.mult))
        es1.close()

        es2 = es.enter_context(ExitStack())
        sb2 = cx.mk(es2)
        wg = sb2("wg", [128, NC * 1024], BF16)
        wu = sb2("wu", [128, NC * 1024], BF16)
        ATS = sb2("ATS", [128, NC * 2048], BF16)
        load_w_bf16(cx, sgT, wg, D, 1024)
        load_w_bf16(cx, suT, wu, D, 1024)
        for tc in range(4):
            for og in range(8):
                for j in range(NC):
                    p.em("tensor", lambda e, tc=tc, og=og, j=j: e.matmul(
                        cx.psA[:, :], wg[:, j * 1024 + og * 128: j * 1024 + (og + 1) * 128],
                        zT[j][:, tc * 512:(tc + 1) * 512], start=(j == 0), stop=(j == NC - 1)))
                for j in range(NC):
                    p.em("tensor", lambda e, tc=tc, og=og, j=j: e.matmul(
                        cx.psB[:, :], wu[:, j * 1024 + og * 128: j * 1024 + (og + 1) * 128],
                        zT[j][:, tc * 512:(tc + 1) * 512], start=(j == 0), stop=(j == NC - 1)))
                p.em("scalar", lambda e: e.activation(cx.scA[:, 0:512], cx.psA[:, :], AF.Silu))
                p.em("vector", lambda e, og=og, tc=tc: e.tensor_mul(
                    ATS[:, og * 2048 + tc * 512: og * 2048 + (tc + 1) * 512],
                    cx.psB[:, :], cx.scA[:, 0:512]))
        load_w_bf16(cx, sdT, wg, D, 1024)
        for i in range(NT):
            for ch in range(2):
                for j in range(NC):
                    p.em("tensor", lambda e, i=i, ch=ch, j=j: e.matmul(
                        cx.psA[:, :], ATS[:, j * 2048 + i * 128: j * 2048 + (i + 1) * 128],
                        wg[:, j * 1024 + ch * 512: j * 1024 + (ch + 1) * 512],
                        start=(j == 0), stop=(j == NC - 1)))
                p.em("vector", lambda e, i=i, ch=ch: e.tensor_add(
                    ACC[:, i * 1024 + ch * 512: i * 1024 + (ch + 1) * 512],
                    ACC[:, i * 1024 + ch * 512: i * 1024 + (ch + 1) * 512], cx.psA[:, :]))
        es2.close()

        es3 = es.enter_context(ExitStack())
        sb3 = cx.mk(es3)
        wge = [sb3(f"wge{k}", [128, NC * 1024], BF16) for k in range(2)]
        wd = [sb3(f"wd{k}", [128, 4 * 1024], BF16) for k in range(2)]
        wstg = sb3("wstg", [128, 1024], F32)
        AT = sb3("AT", [128, 4 * 2048], BF16)
        exmarks = {}
        exmarks[-1] = p.mark("T")

        def prefetch(ex):
            k = ex % 2
            wT = exmarks[ex - 1]
            for j in range(NC + 4):
                src = guT[ex][j * 128:(j + 1) * 128, :] if j < NC else                     dnT[ex][(j - NC) * 128:(j - NC + 1) * 128, :]
                dst = wge[k][:, j * 1024:(j + 1) * 1024] if j < NC else                     wd[k][:, (j - NC) * 1024:(j - NC + 1) * 1024]
                p.dma("sync", wstg[:, :], src, sem="P",
                      waits={"T": wT, "P": p.mark("P")})
                p.em("vector", lambda e, dst=dst: e.tensor_copy(out=dst, in_=wstg[:, :]),
                     sem="P", waits={"P": p.mark("P")})

        for ex in range(32):
            prefetch(ex)
            pAfter = p.mark("P")
            exmarks[ex] = p.mark("T")
            k = ex % 2
            first_mm = [True]
            def mmw():
                if first_mm[0]:
                    first_mm[0] = False
                    return {"T": p.mark("T"), "P": pAfter}
                return None
            for tc in range(4):
                for og in range(4):
                    for j in range(NC):
                        p.em("tensor", lambda e, og=og, tc=tc, j=j, k=k: e.matmul(
                            cx.psA[:, :], wge[k][:, j * 1024 + og * 128: j * 1024 + (og + 1) * 128],
                            zT[j][:, tc * 512:(tc + 1) * 512], start=(j == 0), stop=(j == NC - 1)),
                            waits=mmw())
                    for j in range(NC):
                        p.em("tensor", lambda e, og=og, tc=tc, j=j, k=k: e.matmul(
                            cx.psB[:, :], wge[k][:, j * 1024 + (og + 4) * 128: j * 1024 + (og + 5) * 128],
                            zT[j][:, tc * 512:(tc + 1) * 512], start=(j == 0), stop=(j == NC - 1)))
                    p.em("scalar", lambda e: e.activation(cx.scA[:, 0:512], cx.psA[:, :], AF.Silu))
                    p.em("vector", lambda e, og=og, tc=tc: e.tensor_mul(
                        AT[:, og * 2048 + tc * 512: og * 2048 + (tc + 1) * 512],
                        cx.psB[:, :], cx.scA[:, 0:512]))
            for i in range(NT):
                for ch in range(2):
                    for j in range(4):
                        p.em("tensor", lambda e, i=i, ch=ch, j=j, k=k: e.matmul(
                            cx.psA[:, :], AT[:, j * 2048 + i * 128: j * 2048 + (i + 1) * 128],
                            wd[k][:, j * 1024 + ch * 512: j * 1024 + (ch + 1) * 512],
                            start=(j == 0), stop=(j == 3)))
                    p.em("vector", lambda e, i=i, ch=ch, ex=ex: e.scalar_tensor_tensor(
                        out=ACC[:, i * 1024 + ch * 512: i * 1024 + (ch + 1) * 512],
                        in0=cx.psA[:, :], scalar=combine[:, i * 32 + ex: i * 32 + ex + 1],
                        in1=ACC[:, i * 1024 + ch * 512: i * 1024 + (ch + 1) * 512],
                        op0=ALU.mult, op1=ALU.add))
        es3.close()

        for i in range(NT):
            p.dma("gpsimd", cx.xtile[:, :], xin[i * 128:(i + 1) * 128, :])
            p.em("vector", lambda e, i=i: e.scalar_tensor_tensor(
                out=cx.xtile[:, :], in0=ACC[:, i * 1024:(i + 1) * 1024], scalar=0.5,
                in1=cx.xtile[:, :], op0=ALU.mult, op1=ALU.add))
            p.dma("gpsimd", xout[i * 128:(i + 1) * 128, :], cx.xtile[:, :])



def build_l1ffn():
    nc = bass.Bass("TRN2", target_bir_lowering=False)
    p = Prog()
    es = ExitStack()
    with es:
        cx = Ctx(nc, p, es)
        cx.xin = cx.inp("xin", [T, D])
        cx.xout = nc.dram_tensor("xout", [T, D], F32, kind="ExternalOutput").ap()
        cx.setup_common()
        cx.cur = es.enter_context(ExitStack())
        emit_l1ffn(cx)
        sem = es.enter_context(nc.semaphore("tick"))
        semP = es.enter_context(nc.semaphore("pref"))
        block = es.enter_context(nc.Block())
        p.finalize(nc, block, {"T": sem, "P": semP})
    return nc


def l1ffn_inputs(full, x):
    c = host_consts()
    return {"xin": x,
            "gwT": np.ascontiguousarray(full["L1_ffn_gate_weight"].T),
            "ebias": np.ascontiguousarray(full["L1_ffn_e_score_correction_bias"][None, :]),
            "guT": np.ascontiguousarray(full["L1_ffn_experts_gate_up"].transpose(0, 2, 1)),
            "dnT": np.ascontiguousarray(full["L1_ffn_experts_down"].transpose(0, 2, 1)),
            "sgT": np.ascontiguousarray(full["L1_ffn_shared_gate_w"].T),
            "suT": np.ascontiguousarray(full["L1_ffn_shared_up_w"].T),
            "sdT": np.ascontiguousarray(full["L1_ffn_shared_down_w"].T),
            "identF": c["ident"], "m0tF": c["m0t"], "triF": c["tri01"]}


def emit_l1attn(cx):
    nc, p, es = cx.nc, cx.p, cx.cur
    if True:
        xin = cx.xin
        Wq = cx.inp("Wq", [D, D]); Wk = cx.inp("Wk", [D, D]); Wv = cx.inp("Wv", [D, D])
        Wg1 = cx.inp("Wg1", [D, 16]); Wg2 = cx.inp("Wg2", [16, D])
        bg2r = cx.inp("bg2r", [1, D]); gnwr = cx.inp("gnwr", [1, D])
        Wg = cx.inp("Wg", [D, D]); Wo = cx.inp("Wo", [D, D])
        xout = cx.xout

        zT = [cx.sb(f"zT{j}", [128, T], BF16) for j in range(NC)]
        emit_ln_zT(cx, xin, zT, rms=False)

        wq = cx.sb("wq", [128, NC * 1024], BF16)
        wk = cx.sb("wk", [128, NC * 1024], BF16)
        wv = cx.sb("wv", [128, NC * 1024], BF16)
        wgt = cx.sb("wgt", [128, NC * 1024], BF16)
        wo = cx.sb("wo", [128, NC * 1024], BF16)
        for hbm, slab in ((Wq, wq), (Wk, wk), (Wv, wv), (Wg, wgt), (Wo, wo)):
            load_w_bf16(cx, hbm, slab, D, 1024)
        wg1 = cx.sb("wg1", [128, NC * 16], BF16)
        for j in range(NC):
            p.dma("sync", cx.scA[:, 0:16], Wg1[j * 128:(j + 1) * 128, :])
            p.em("vector", lambda e, j=j: e.tensor_copy(out=wg1[:, j * 16:(j + 1) * 16],
                                                        in_=cx.scA[:, 0:16]))
        wg2b = cx.sb("wg2b", [16, 1024], BF16)
        p.dma("sync", cx.scA[0:16, 0:1024], Wg2)
        p.em("vector", lambda e: e.tensor_copy(out=wg2b[:, :], in_=cx.scA[0:16, 0:1024]))
        bg2b = cx.sb("bg2b", [128, 1024], F32)
        gnwb = cx.sb("gnwb", [128, 1024], F32)
        rowt = cx.sb("rowt", [1, 1024], F32)
        for src, dst in ((bg2r, bg2b), (gnwr, gnwb)):
            p.dma("sync", rowt[:, :], src)
            for ch in range(2):
                p.em("tensor", lambda e, ch=ch: e.matmul(cx.psA[:, :], cx.sOnes[0:1, 0:128],
                                                         rowt[0:1, ch * 512:(ch + 1) * 512],
                                                         start=True, stop=True))
                p.em("scalar", lambda e, ch=ch, dst=dst: e.copy(out=dst[:, ch * 512:(ch + 1) * 512],
                                                                in_=cx.psA[:, :]))

        S = cx.sb("S", [64, 16 * 64], F32)
        Sb = cx.sb("Sb", [128, 16 * 64], BF16)
        eaTh = cx.sb("eaTh", [64, 16], F32)
        p.em("vector", lambda e: e.memset(S[:, :], 0.0))
        p.em("vector", lambda e: e.memset(Sb[:, :], 0.0))
        qf = cx.sb("qf", [128, 1024], F32)
        kf = cx.sb("kf", [128, 1024], F32)
        vb = cx.sb("vb", [128, 1024], BF16)
        gatef = cx.sb("gatef", [128, 1024], F32)
        gkt = cx.sb("gkt", [128, 1024], F32)
        acum = cx.sb("acum", [128, 1024], F32)
        arow = cx.sb("arow", [1, 1024], F32)
        eaT = cx.sb("eaT", [128, 16], F32)
        g1t = cx.sb("g1t", [16, 128], BF16)
        qbB = cx.sb("qbB", [128, 1024], BF16)
        kbB = cx.sb("kbB", [128, 1024], BF16)
        qTc = cx.sb("qTc", [128, NC * 128], BF16)
        kTc = cx.sb("kTc", [128, NC * 128], BF16)
        of = cx.sb("of", [128, 1024], F32)
        pb = cx.sb("pb", [128, 128], BF16)
        ogb = cx.sb("ogb", [128, 1024], BF16)
        ogT = cx.sb("ogT", [128, NC * 128], BF16)

        def proj(dst_ps, wslab, i, ch):
            for j in range(NC):
                p.em("tensor", lambda e, j=j, i=i, ch=ch, w=wslab, d=dst_ps: e.matmul(
                    d[:, :], zT[j][:, i * 128:(i + 1) * 128],
                    w[:, j * 1024 + ch * 512: j * 1024 + (ch + 1) * 512],
                    start=(j == 0), stop=(j == NC - 1)))

        for i in range(NT):
            for dst, wslab in ((qf, wq), (kf, wk), (gatef, wgt)):
                for ch in range(2):
                    proj(cx.psA, wslab, i, ch)
                    p.em("scalar", lambda e, ch=ch, dst=dst: e.copy(
                        out=dst[:, ch * 512:(ch + 1) * 512], in_=cx.psA[:, :]))
            for ch in range(2):
                proj(cx.psA, wv, i, ch)
                p.em("scalar", lambda e, ch=ch: e.copy(out=vb[:, ch * 512:(ch + 1) * 512],
                                                       in_=cx.psA[:, :]))
            for j in range(NC):
                p.em("tensor", lambda e, j=j, i=i: e.matmul(
                    cx.psB[:, 0:16], zT[j][:, i * 128:(i + 1) * 128], wg1[:, j * 16:(j + 1) * 16],
                    start=(j == 0), stop=(j == NC - 1)))
            p.em("scalar", lambda e: e.copy(out=cx.scA[:, 0:16], in_=cx.psB[:, 0:16]))
            p.em("vector", lambda e: e.tensor_copy(out=cx.bB[:, 0:16], in_=cx.scA[:, 0:16]))
            cx.tr(cx.bB[:, 0:16], g1t[0:16, 0:128], bf16=True, np_=128, nf=16)
            for ch in range(2):
                p.em("tensor", lambda e, ch=ch: e.matmul(
                    cx.psA[:, :], g1t[0:16, 0:128], wg2b[0:16, ch * 512:(ch + 1) * 512],
                    start=True, stop=True))
                p.em("vector", lambda e, ch=ch: e.tensor_add(
                    gkt[:, ch * 512:(ch + 1) * 512], bg2b[:, ch * 512:(ch + 1) * 512], cx.psA[:, :]))
            p.em("vector", lambda e: e.tensor_scalar_min(cx.scA[:, :], gkt[:, :], 0.0))
            p.em("scalar", lambda e: e.activation(cx.scB[:, :], gkt[:, :], AF.Abs))
            p.em("scalar", lambda e: e.activation(cx.scB[:, :], cx.scB[:, :], AF.Exp, scale=-1.0))
            p.em("scalar", lambda e: e.activation(cx.scB[:, :], cx.scB[:, :], AF.Ln, bias=1.0))
            p.em("vector", lambda e: e.tensor_sub(cx.scA[:, :], cx.scA[:, :], cx.scB[:, :]))
            p.em("vector", lambda e: e.tensor_scalar_mul(gkt[:, :], cx.scA[:, :], 1.0 / 16.0))
            for ch in range(2):
                p.em("tensor", lambda e, ch=ch: e.matmul(
                    cx.psA[:, :], cx.sTri[:, :], gkt[:, ch * 512:(ch + 1) * 512],
                    start=True, stop=True))
                p.em("scalar", lambda e, ch=ch: e.copy(out=acum[:, ch * 512:(ch + 1) * 512],
                                                       in_=cx.psA[:, :]))
            p.dma("gpsimd", arow[:, :], acum[127:128, :])
            for j in range(NC):
                p.dma("gpsimd", eaT[:, j:j + 1], arow[0:1, j * 128:(j + 1) * 128])
            p.em("scalar", lambda e: e.activation(eaT[:, 8:16], eaT[:, 0:8], AF.Exp))
            p.em("scalar", lambda e: e.activation(cx.scA[:, :], acum[:, :], AF.Exp))
            p.em("vector", lambda e: e.tensor_mul(qf[:, :], qf[:, :], cx.scA[:, :]))
            p.em("vector", lambda e: e.tensor_scalar_mul(qbB[:, :], qf[:, :], 0.125))
            p.em("scalar", lambda e: e.activation(cx.scB[:, :], acum[:, :], AF.Exp, scale=-1.0))
            p.em("vector", lambda e: e.tensor_mul(kf[:, :], kf[:, :], cx.scB[:, :]))
            p.em("vector", lambda e: e.tensor_copy(out=kbB[:, :], in_=kf[:, :]))
            for j in range(NC):
                cx.tr(qbB[:, j * 128:(j + 1) * 128], qTc[:, j * 128:(j + 1) * 128])
                cx.tr(kbB[:, j * 128:(j + 1) * 128], kTc[:, j * 128:(j + 1) * 128])
            for h in range(16):
                jt, off = h // 2, (h % 2) * 64
                p.em("tensor", lambda e, jt=jt, off=off: e.matmul(
                    cx.psA[:, 0:128], kTc[off:off + 64, jt * 128:(jt + 1) * 128],
                    qTc[off:off + 64, jt * 128:(jt + 1) * 128], start=True, stop=True))
                p.em("vector", lambda e: e.tensor_mul(pb[:, :], cx.psA[:, 0:128], cx.sTri[:, :]))
                p.em("tensor", lambda e, h=h: e.matmul(
                    cx.psB[:, 0:64], pb[:, :], vb[:, h * 64:(h + 1) * 64], start=True, stop=False))
                p.em("tensor", lambda e, h=h, jt=jt, off=off: e.matmul(
                    cx.psB[:, 0:64], qTc[off:off + 64, jt * 128:(jt + 1) * 128],
                    Sb[off:off + 64, h * 64:(h + 1) * 64], start=False, stop=True))
                p.em("scalar", lambda e, h=h: e.copy(out=of[:, h * 64:(h + 1) * 64],
                                                     in_=cx.psB[:, 0:64]))
                p.em("tensor", lambda e, h=h: e.matmul(
                    cx.psO[0:64, 0:64], kbB[:, h * 64:(h + 1) * 64], vb[:, h * 64:(h + 1) * 64],
                    start=True, stop=True))
                p.em("vector", lambda e, h=h: e.tensor_add(
                    cx.scB[0:64, 0:64], cx.psO[0:64, 0:64], S[0:64, h * 64:(h + 1) * 64]))
                p.em("vector", lambda e, h=h, jt=jt, off=off: e.tensor_copy(
                    out=eaTh[0:64, h:h + 1], in_=eaT[off:off + 64, 8 + jt:9 + jt]))
                p.em("vector", lambda e, h=h: e.tensor_scalar_mul(
                    S[0:64, h * 64:(h + 1) * 64], cx.scB[0:64, 0:64], eaTh[0:64, h:h + 1]))
                p.em("vector", lambda e, h=h: e.tensor_copy(out=Sb[0:64, h * 64:(h + 1) * 64],
                                                            in_=S[0:64, h * 64:(h + 1) * 64]))
                p.em("vector", lambda e, h=h: e.tensor_copy(out=Sb[64:128, h * 64:(h + 1) * 64],
                                                            in_=S[0:64, h * 64:(h + 1) * 64]))
            p.em("scalar", lambda e: e.activation(cx.scB[:, :], of[:, :], AF.Square))
            p.em("vector", lambda e: e.reduce_sum(cx.st[:, 8:24],
                                                  cx.scB[:, :].rearrange("p (h d) -> p h d", h=16),
                                                  axis=AX.X))
            p.em("vector", lambda e: e.tensor_scalar(out=cx.st[:, 8:24], in0=cx.st[:, 8:24],
                                                     scalar1=1.0 / 64, scalar2=1e-5,
                                                     op0=ALU.mult, op1=ALU.add))
            p.em("scalar", lambda e: e.sqrt(cx.st[:, 24:40], cx.st[:, 8:24]))
            p.em("vector", lambda e: e.reciprocal(cx.st[:, 40:56], cx.st[:, 24:40]))
            for h in range(16):
                p.em("vector", lambda e, h=h: e.tensor_scalar_mul(
                    of[:, h * 64:(h + 1) * 64], of[:, h * 64:(h + 1) * 64], cx.st[:, 40 + h:41 + h]))
            p.em("vector", lambda e: e.tensor_mul(of[:, :], of[:, :], gnwb[:, :]))
            p.em("scalar", lambda e: e.activation(cx.scB[:, :], gatef[:, :], AF.Sigmoid))
            p.em("vector", lambda e: e.tensor_mul(gatef[:, :], gatef[:, :], cx.scB[:, :]))
            p.em("vector", lambda e: e.tensor_mul(of[:, :], of[:, :], gatef[:, :]))
            p.em("vector", lambda e: e.tensor_copy(out=ogb[:, :], in_=of[:, :]))
            for j in range(NC):
                cx.tr(ogb[:, j * 128:(j + 1) * 128], ogT[:, j * 128:(j + 1) * 128])
            p.dma("gpsimd", cx.xtile[:, :], xin[i * 128:(i + 1) * 128, :])
            for ch in range(2):
                for j in range(NC):
                    p.em("tensor", lambda e, j=j, ch=ch: e.matmul(
                        cx.psA[:, :], ogT[:, j * 128:(j + 1) * 128],
                        wo[:, j * 1024 + ch * 512: j * 1024 + (ch + 1) * 512],
                        start=(j == 0), stop=(j == NC - 1)))
                p.em("vector", lambda e, ch=ch: e.tensor_add(
                    cx.xtile[:, ch * 512:(ch + 1) * 512], cx.xtile[:, ch * 512:(ch + 1) * 512],
                    cx.psA[:, :]))
            p.dma("gpsimd", xout[i * 128:(i + 1) * 128, :], cx.xtile[:, :])



def build_l1attn():
    nc = bass.Bass("TRN2", target_bir_lowering=False)
    p = Prog()
    es = ExitStack()
    with es:
        cx = Ctx(nc, p, es)
        cx.xin = cx.inp("xin", [T, D])
        cx.xout = nc.dram_tensor("xout", [T, D], F32, kind="ExternalOutput").ap()
        cx.setup_common()
        cx.cur = es.enter_context(ExitStack())
        emit_l1attn(cx)
        sem = es.enter_context(nc.semaphore("tick"))
        semP = es.enter_context(nc.semaphore("pref"))
        block = es.enter_context(nc.Block())
        p.finalize(nc, block, {"T": sem, "P": semP})
    return nc


def l1attn_inputs(full, x):
    c = host_consts()
    return {"xin": x,
            "Wq": full["L1_attn_Wq"], "Wk": full["L1_attn_Wk"], "Wv": full["L1_attn_Wv"],
            "Wg1": full["L1_attn_Wg1"], "Wg2": full["L1_attn_Wg2"],
            "bg2r": np.ascontiguousarray(full["L1_attn_bg2"][None, :]),
            "gnwr": np.ascontiguousarray(np.tile(full["L1_attn_g_norm_w"], 16)[None, :]),
            "Wg": full["L1_attn_Wg"], "Wo": full["L1_attn_Wo"],
            "identF": c["ident"], "m0tF": c["m0t"], "triF": c["tri01"]}

def _routing(cx, lg_hbm, iota8s, idxi, cB, tag):
    p = cx.p
    lg = cx.st[0:1, 16:24]
    rk = cx.st[0:1, 24:32]
    t8 = cx.st[0:1, 32:40]
    ex8 = cx.st[0:1, 40:48]
    cw = cx.st[0:1, 48:56]
    crow = cx.st[0:1, 56:58]
    p.dma("sync", lg, lg_hbm)
    for b in range(8):
        p.em("vector", lambda e, b=b: e.tensor_single_scalar(out=t8, in_=lg,
                                                             scalar=lg[0:1, b:b + 1], op=ALU.is_gt))
        p.em("vector", lambda e, b=b: e.reduce_sum(rk[0:1, b:b + 1], t8, axis=AX.X))
    p.em("vector", lambda e: e.reduce_max(cx.st[0:1, 58:59], lg, axis=AX.X))
    p.em("vector", lambda e: e.tensor_scalar_mul(cx.st[0:1, 58:59], cx.st[0:1, 58:59], -1.0))
    p.em("scalar", lambda e: e.activation(ex8, lg, AF.Exp, bias=cx.st[0:1, 58:59]))
    p.em("vector", lambda e: e.tensor_single_scalar(out=t8, in_=rk, scalar=1.5, op=ALU.is_lt))
    p.em("vector", lambda e: e.tensor_mul(ex8, ex8, t8))
    p.em("vector", lambda e: e.reduce_sum(cx.st[0:1, 59:60], ex8, axis=AX.X))
    p.em("vector", lambda e: e.reciprocal(cx.st[0:1, 60:61], cx.st[0:1, 59:60]))
    p.em("vector", lambda e: e.tensor_scalar_mul(cw, ex8, cx.st[0:1, 60:61]))
    for s in range(2):
        p.em("vector", lambda e, s=s: e.tensor_single_scalar(out=t8, in_=rk,
                                                             scalar=float(s) + 0.5 if False else float(s),
                                                             op=ALU.is_equal))
        p.em("vector", lambda e: e.tensor_mul(ex8, t8, iota8s[0:1, 0:8]))
        p.em("vector", lambda e, s=s: e.reduce_sum(cx.st[0:1, 61 + s:62 + s], ex8, axis=AX.X))
        p.em("vector", lambda e, s=s: e.tensor_copy(out=idxi[0:1, s:s + 1],
                                                    in_=cx.st[0:1, 61 + s:62 + s]))
        p.em("vector", lambda e: e.tensor_mul(t8, t8, cw))
        p.em("vector", lambda e, s=s: e.reduce_sum(crow[0:1, s:s + 1], t8, axis=AX.X))
    cx.bcast_row(crow, cB[:, 0:2], 128, 2)


def emit_l0ffn(cx):
    nc, p, es = cx.nc, cx.p, cx.cur
    if True:
        xin = cx.xin
        wqT = cx.inp("wqT", [D, D])
        wkT = cx.inp("wkT", [D, 256])
        wvT = cx.inp("wvT", [D, 256])
        woT = cx.inp("woT", [D, D])
        qg16r = cx.inp("qg16r", [1, 16])
        kg16r = cx.inp("kg16r", [1, 16])
        kg4r = cx.inp("kg4r", [1, 4])
        fc_s1 = cx.inp("fc_s1", [8, 1024, 64])
        fc_s2 = cx.inp("fc_s2", [8, 4096, 64])
        fcmixT2 = cx.inp("fcmixT2", [8, 2048, 4096])
        fc_lg = cx.inp("fc_lg", [1, 8])
        pj_s1 = cx.inp("pj_s1", [8, 4096, 64])
        pj_s2 = cx.inp("pj_s2", [8, 1024, 64])
        pjmixT2 = cx.inp("pjmixT2", [8, 8192, 1024])
        pj_lg = cx.inp("pj_lg", [1, 8])
        r0c = cx.inp("r0c", [T, D])
        r0s = cx.inp("r0s", [T, D])
        iota8f = cx.inp("iota8f", [1, 8])
        xout = cx.xout
        yscr = cx.internal("yscr", [T, D])
        fcscr = cx.internal("fcscr", [2, 1024, 4096])
        pjscr = cx.internal("pjscr", [2, 4096, 1024])
        s1fscr = cx.internal("s1fscr", [2, 1024, 64])
        s2fscr = cx.internal("s2fscr", [2, 4096, 64])
        s1pscr = cx.internal("s1pscr", [2, 4096, 64])
        s2pscr = cx.internal("s2pscr", [2, 1024, 64])
        iota8s = cx.sb("iota8s", [1, 8], F32)
        p.dma("sync", iota8s[:, :], iota8f)

        hT = [cx.sb(f"hT{j}", [128, T], BF16) for j in range(NC)]

        esA = es.enter_context(ExitStack())
        sbA = cx.mk(esA)
        ytile = sbA("ytile", [128, 1024], F32)
        qwork = sbA("qwork", [128, 1024], F32)
        rwork = sbA("rwork", [128, 1024], F32)
        ctile = sbA("ctile", [128, 1024], F32)
        stile = sbA("stile", [128, 1024], F32)
        qT = [sbA(f"qT{j}", [128, T], BF16) for j in range(NC)]
        kT2 = [sbA(f"kT2{j}", [128, T], BF16) for j in range(2)]
        kTs = [sbA(f"kTs{j}", [128, T], BF16) for j in range(2)]
        vext4 = sbA("vext4", [128, NT * 4 * 65], BF16)
        qgb = sbA("qgb", [128, 16], F32)
        kgb4 = sbA("kgb4", [128, 4], F32)
        mneg = sbA("mneg", [128, 16], F32)
        grow = sbA("grow", [1, 16], F32)
        g4 = sbA("g4", [1, 4], F32)
        p.dma("sync", grow[:, :], qg16r)
        cx.bcast_row(grow[0:1, 0:16], qgb[:, :], 128, 16)
        p.dma("sync", g4[:, :], kg4r)
        cx.bcast_row(g4[0:1, 0:4], kgb4[:, :], 128, 4)
        p.dma("sync", cx.st[0:1, 16:32], kg16r)
        p.em("vector", lambda e: e.tensor_mul(grow[0:1, 0:16], grow[0:1, 0:16], cx.st[0:1, 16:32]))
        p.em("vector", lambda e: e.tensor_scalar_mul(grow[0:1, 0:16], grow[0:1, 0:16], -8.0))
        cx.bcast_row(grow[0:1, 0:16], mneg[:, :], 128, 16)

        esA1 = esA.enter_context(ExitStack())
        sbA1 = cx.mk(esA1)
        zT = [sbA1(f"zT{j}", [128, T], BF16) for j in range(NC)]
        wslab = sbA1("wslab", [128, NC * 1024], BF16)
        for i in range(NT):
            p.dma("gpsimd", cx.xtile[:, :], xin[i * 128:(i + 1) * 128, :])
            cx.norm_tile(cx.xtile[:, :], ytile[:, :], None, False, 1e-5)
            p.dma("gpsimd", yscr[i * 128:(i + 1) * 128, :], ytile[:, :])
            cx.norm_tile(ytile[:, :], None, cx.bB[:, :], True, 1e-6)
            for j in range(NC):
                cx.tr(cx.bB[:, j * 128:(j + 1) * 128], zT[j][:, i * 128:(i + 1) * 128])

        def headnorm_rope(buf, nh, gb, eps):
            w = nh * 64
            p.em("scalar", lambda e: e.activation(cx.scB[:, 0:w], buf[:, 0:w], AF.Square))
            p.em("vector", lambda e: e.reduce_sum(cx.st[:, 8:8 + nh],
                                                  buf[:, 0:w].rearrange("p (h d) -> p h d", d=64) if False else
                                                  cx.scB[:, 0:w].rearrange("p (h d) -> p h d", d=64),
                                                  axis=AX.X))
            p.em("vector", lambda e: e.tensor_scalar(out=cx.st[:, 8:8 + nh], in0=cx.st[:, 8:8 + nh],
                                                     scalar1=1.0 / 64, scalar2=eps,
                                                     op0=ALU.mult, op1=ALU.add))
            p.em("scalar", lambda e: e.sqrt(cx.st[:, 24:24 + nh], cx.st[:, 8:8 + nh]))
            p.em("vector", lambda e: e.reciprocal(cx.st[:, 40:40 + nh], cx.st[:, 24:24 + nh]))
            p.em("vector", lambda e, gb=gb: e.tensor_mul(cx.st[:, 40:40 + nh], cx.st[:, 40:40 + nh],
                                                         gb[:, 0:nh]))
            for h in range(nh):
                p.em("vector", lambda e, h=h: e.tensor_scalar_mul(
                    buf[:, h * 64:(h + 1) * 64], buf[:, h * 64:(h + 1) * 64], cx.st[:, 40 + h:41 + h]))
            bv = buf[:, 0:w].rearrange("p (h two f) -> p h two f", two=2, f=32)
            rv = rwork[:, 0:w].rearrange("p (h two f) -> p h two f", two=2, f=32)
            p.em("vector", lambda e: e.tensor_copy(out=rv[:, :, 0, :], in_=bv[:, :, 1, :]))
            p.em("vector", lambda e: e.tensor_copy(out=rv[:, :, 1, :], in_=bv[:, :, 0, :]))
            p.em("vector", lambda e, w=w: e.tensor_mul(buf[:, 0:w], buf[:, 0:w], ctile[:, 0:w]))
            p.em("vector", lambda e, w=w: e.tensor_mul(rwork[:, 0:w], rwork[:, 0:w], stile[:, 0:w]))
            p.em("vector", lambda e, w=w: e.tensor_add(buf[:, 0:w], buf[:, 0:w], rwork[:, 0:w]))

        load_w_bf16(cx, wqT, wslab, D, 1024)
        for i in range(NT):
            p.dma("gpsimd", ctile[:, :], r0c[i * 128:(i + 1) * 128, :])
            p.dma("gpsimd", stile[:, :], r0s[i * 128:(i + 1) * 128, :])
            for ch in range(2):
                for j in range(NC):
                    p.em("tensor", lambda e, i=i, j=j, ch=ch: e.matmul(
                        cx.psA[:, :], zT[j][:, i * 128:(i + 1) * 128],
                        wslab[:, j * 1024 + ch * 512: j * 1024 + (ch + 1) * 512],
                        start=(j == 0), stop=(j == NC - 1)))
                p.em("scalar", lambda e, ch=ch: e.copy(out=qwork[:, ch * 512:(ch + 1) * 512],
                                                       in_=cx.psA[:, :]))
            headnorm_rope(qwork, 16, qgb, 1e-6)
            p.em("vector", lambda e: e.tensor_copy(out=cx.bB[:, :], in_=qwork[:, :]))
            for j in range(NC):
                cx.tr(cx.bB[:, j * 128:(j + 1) * 128], qT[j][:, i * 128:(i + 1) * 128])
        load_w_bf16(cx, wkT, wslab, D, 256)
        for i in range(NT):
            p.dma("gpsimd", ctile[:, 0:256], r0c[i * 128:(i + 1) * 128, 0:256])
            p.dma("gpsimd", stile[:, 0:256], r0s[i * 128:(i + 1) * 128, 0:256])
            for j in range(NC):
                p.em("tensor", lambda e, i=i, j=j: e.matmul(
                    cx.psA[:, 0:256], zT[j][:, i * 128:(i + 1) * 128],
                    wslab[:, j * 256:(j + 1) * 256], start=(j == 0), stop=(j == NC - 1)))
            p.em("scalar", lambda e: e.copy(out=qwork[:, 0:256], in_=cx.psA[:, 0:256]))
            headnorm_rope(qwork, 4, kgb4, 1e-6)
            p.em("vector", lambda e: e.tensor_copy(out=cx.bB[:, 0:256], in_=qwork[:, 0:256]))
            for j in range(2):
                cx.tr(cx.bB[:, j * 128:(j + 1) * 128], kT2[j][:, i * 128:(i + 1) * 128])
        for j in range(2):
            p.em("vector", lambda e, j=j: e.tensor_copy(out=kTs[j][0:64, :], in_=kT2[j][64:128, :]))
            p.em("vector", lambda e, j=j: e.tensor_copy(out=kTs[j][64:128, :], in_=kT2[j][0:64, :]))
        load_w_bf16(cx, wvT, wslab, D, 256)
        for i in range(NT):
            for j in range(NC):
                p.em("tensor", lambda e, i=i, j=j: e.matmul(
                    cx.psA[:, 0:256], zT[j][:, i * 128:(i + 1) * 128],
                    wslab[:, j * 256:(j + 1) * 256], start=(j == 0), stop=(j == NC - 1)))
            dst = vext4[:, i * 4 * 65:(i + 1) * 4 * 65].rearrange("p (h w) -> p h w", w=65)
            p.em("vector", lambda e, dst=dst: e.tensor_copy(
                out=dst[:, :, 0:64], in_=cx.psA[:, 0:256].rearrange("p (h d) -> p h d", d=64)))
            p.em("vector", lambda e, dst=dst: e.memset(dst[:, :, 64], 1.0))
        esA1.close()

        attT = [sbA(f"attT{j}", [128, T], BF16) for j in range(NC)]
        pexp = sbA("pexp", [128, NT * 512], BF16)
        rrow = sbA("rrow", [1, 512], F32)
        rb = sbA("rb", [64, 512], F32)
        wslab2 = sbA("wslab2", [128, NC * 1024], BF16)
        for h in range(16):
            jt, off = h // 2, (h % 2) * 64
            kv = h // 4
            kvt, kvo = kv // 2, (kv % 2) * 64
            ksl = kT2[kvt] if kvo == off else kTs[kvt]
            for c in range(4):
                ntk = 4 * c + 4
                for kt in range(ntk):
                    p.em("tensor", lambda e, kt=kt, c=c, ksl=ksl, off=off, jt=jt: e.matmul(
                        cx.psA[:, :], ksl[off:off + 64, kt * 128:(kt + 1) * 128],
                        qT[jt][off:off + 64, c * 512:(c + 1) * 512], start=True, stop=True))
                    q4d = kt - 4 * c
                    if q4d < 0:
                        p.em("scalar", lambda e, kt=kt, h=h: e.activation(
                            pexp[:, kt * 512:(kt + 1) * 512], cx.psA[:, :], AF.Exp,
                            bias=mneg[:, h:h + 1], scale=0.125))
                    else:
                        if q4d > 0:
                            p.em("vector", lambda e, kt=kt, q4d=q4d: e.memset(
                                pexp[:, kt * 512: kt * 512 + q4d * 128], 0.0))
                        p.em("vector", lambda e, q4d=q4d: e.tensor_add(
                            cx.scB[:, 0:128], cx.psA[:, q4d * 128:(q4d + 1) * 128], cx.sM0[:, :]))
                        p.em("scalar", lambda e, kt=kt, q4d=q4d, h=h: e.activation(
                            pexp[:, kt * 512 + q4d * 128: kt * 512 + (q4d + 1) * 128],
                            cx.scB[:, 0:128], AF.Exp, bias=mneg[:, h:h + 1], scale=0.125))
                        if q4d < 3:
                            p.em("scalar", lambda e, kt=kt, q4d=q4d, h=h: e.activation(
                                pexp[:, kt * 512 + (q4d + 1) * 128:(kt + 1) * 512],
                                cx.psA[:, (q4d + 1) * 128:], AF.Exp,
                                bias=mneg[:, h:h + 1], scale=0.125))
                for kt in range(ntk):
                    p.em("tensor", lambda e, kt=kt, kv=kv, ntk=ntk: e.matmul(
                        cx.psO[0:65, :], vext4[:, kt * 4 * 65 + kv * 65: kt * 4 * 65 + (kv + 1) * 65],
                        pexp[:, kt * 512:(kt + 1) * 512], start=(kt == 0), stop=(kt == ntk - 1)))
                p.em("vector", lambda e: e.reciprocal(rrow[0:1, :], cx.psO[64:65, :]))
                p.em("tensor", lambda e: e.matmul(cx.psB[0:64, :], cx.sOnes[0:1, 0:64], rrow[0:1, :],
                                                  start=True, stop=True))
                p.em("scalar", lambda e: e.copy(out=rb[:, :], in_=cx.psB[0:64, :]))
                p.em("vector", lambda e, c=c, jt=jt, off=off: e.tensor_mul(
                    attT[jt][off:off + 64, c * 512:(c + 1) * 512], cx.psO[0:64, :], rb[:, :]))
        load_w_bf16(cx, woT, wslab2, D, 1024)
        for i in range(NT):
            p.dma("gpsimd", ytile[:, :], yscr[i * 128:(i + 1) * 128, :])
            for ch in range(2):
                for j in range(NC):
                    p.em("tensor", lambda e, i=i, j=j, ch=ch: e.matmul(
                        cx.psA[:, :], attT[j][:, i * 128:(i + 1) * 128],
                        wslab2[:, j * 1024 + ch * 512: j * 1024 + (ch + 1) * 512],
                        start=(j == 0), stop=(j == NC - 1)))
                p.em("vector", lambda e, ch=ch: e.tensor_add(
                    ytile[:, ch * 512:(ch + 1) * 512], ytile[:, ch * 512:(ch + 1) * 512], cx.psA[:, :]))
            p.dma("gpsimd", yscr[i * 128:(i + 1) * 128, :], ytile[:, :])
            cx.norm_tile(ytile[:, :], None, cx.bB[:, :], True, 1e-6)
            for j in range(NC):
                cx.tr(cx.bB[:, j * 128:(j + 1) * 128], hT[j][:, i * 128:(i + 1) * 128])
        esA.close()

        esB = es.enter_context(ExitStack())
        sbB = cx.mk(esB)
        idfc = sbB("idfc", [1, 2], I32)
        idpj = sbB("idpj", [1, 2], I32)
        cfcB = sbB("cfcB", [128, 2], F32)
        cpjB = sbB("cpjB", [128, 2], F32)
        _routing(cx, fc_lg, iota8s, idfc, cfcB, "fc")
        _routing(cx, pj_lg, iota8s, idpj, cpjB, "pj")
        if not hasattr(cx, "ffn_regs"):
            cx.ffn_regs = REGS = {}
            def loadregs(e):
                for nm, ap in (("fc0", idfc[0:1, 0:1]), ("fc1", idfc[0:1, 1:2]),
                               ("pj0", idpj[0:1, 0:1]), ("pj1", idpj[0:1, 1:2])):
                    r = e.value_load(ap)
                    REGS[nm] = r
                return e.dma_start(out=cx.st[0:1, 62:63], in_=cx.st[0:1, 63:64])
            p.em("sync", loadregs, inc=16)
            fcm2 = fcmixT2.rearrange("b r c -> (b r) c")
            pjm2 = pjmixT2.rearrange("b r c -> (b r) c")
            for ss in range(2):
                p.em("sync", lambda e, ss=ss: e.dma_start(
                    out=fcscr[ss], in_=fcm2[bass.ds(REGS[f"fc{ss}"] * (2048 + 64), 1024), :]), inc=16)
                p.em("sync", lambda e, ss=ss: e.dma_start(
                    out=pjscr[ss], in_=pjm2[bass.ds(REGS[f"pj{ss}"] * (8192 + 64), 4096), :]), inc=16)
                p.em("sync", lambda e, ss=ss: e.dma_start(
                    out=s1fscr[ss], in_=fc_s1[bass.ds(REGS[f"fc{ss}"], 1)].squeeze(0)), inc=16)
                p.em("sync", lambda e, ss=ss: e.dma_start(
                    out=s2fscr[ss], in_=fc_s2[bass.ds(REGS[f"fc{ss}"], 1)].squeeze(0)), inc=16)
                p.em("sync", lambda e, ss=ss: e.dma_start(
                    out=s1pscr[ss], in_=pj_s1[bass.ds(REGS[f"pj{ss}"], 1)].squeeze(0)), inc=16)
                p.em("sync", lambda e, ss=ss: e.dma_start(
                    out=s2pscr[ss], in_=pj_s2[bass.ds(REGS[f"pj{ss}"], 1)].squeeze(0)), inc=16)
        REGS = cx.ffn_regs
        s1f = [sbB(f"s1f{s}", [128, 1024], BF16) for s in range(2)]
        s2f = [sbB(f"s2f{s}", [128, 4096], BF16) for s in range(2)]
        s1p = [sbB(f"s1p{s}", [128, 4096], BF16) for s in range(2)]
        s2p = [sbB(f"s2p{s}", [128, 1024], BF16) for s in range(2)]

        def gather_sel(rkey, src, dst, ncol, scaleB=None):
            for q in range(ncol // 1024):
                src3 = src.rearrange("(g c) d -> c g d", c=64)
                def gg(e, q=q, src3=src3):
                    return e.dma_start(out=cx.scA[0:64, 0:1024],
                                       in_=src3[:, q * 16:(q + 1) * 16, :])
                p.em("sync", gg, inc=16)
                for base in (0, 64):
                    if scaleB is None:
                        p.em("vector", lambda e, q=q, base=base, dst=dst: e.tensor_copy(
                            out=dst[base:base + 64, q * 1024:(q + 1) * 1024], in_=cx.scA[0:64, 0:1024]))
                    else:
                        p.em("vector", lambda e, q=q, base=base, dst=dst, scaleB=scaleB: e.tensor_scalar_mul(
                            dst[base:base + 64, q * 1024:(q + 1) * 1024], cx.scA[0:64, 0:1024], scaleB))
        for s in range(2):
            gather_sel("", s1fscr[s], s1f[s], 1024)
            gather_sel("", s2fscr[s], s2f[s], 4096, cfcB[0:64, s:s + 1])
            gather_sel("", s1pscr[s], s1p[s], 4096)
            gather_sel("", s2pscr[s], s2p[s], 1024, cpjB[0:64, s:s + 1])

        u1all = sbB("u1all", [128, 32 * 512], BF16)
        u1fc = [u1all[:, ss * 4096:(ss + 1) * 4096] for ss in range(2)]
        u1pj = u1all[:, 0:16384]
        h2T = sbB("h2T", [128, 32 * 512], BF16)
        mslab = sbB("mslab", [128, 32 * 128], BF16)
        mxf = mslab[:, 0:1024]
        mxf2 = mslab[:, 1024:2048]
        mxp = mslab[:, 0:4096]
        m0b = sbB("m0b", [128, 512], BF16)
        m1b = sbB("m1b", [128, 512], BF16)
        mpj = sbB("mpj", [128, 512], BF16)
        otile = [sbB(f"otile{t}", [128, 1024], F32) for t in range(4)]
        fcscr3 = [fcscr[ss].rearrange("(jt p) c -> p jt c", p=128) for ss in range(2)]
        pjscr3 = [pjscr[ss].rearrange("(jt p) c -> p jt c", p=128) for ss in range(2)]

        for tc in range(4):
            for ss in range(2):
                for g in range(16):
                    gb = (g % 2) * 64
                    p.em("tensor", lambda e, ss=ss, g=g, gb=gb, tc=tc: e.matmul(
                        cx.psO[0:64, :], s1f[ss][gb:gb + 64, g * 64:(g + 1) * 64],
                        hT[g // 2][gb:gb + 64, tc * 512:(tc + 1) * 512],
                        start=True, stop=True))
                    p.em("scalar", lambda e, ss=ss, g=g, gb=gb: e.copy(
                        out=u1fc[ss][gb:gb + 64, (g // 2) * 512:(g // 2 + 1) * 512],
                        in_=cx.psO[0:64, :]))
            for ot in range(32):
                for ss, mx in ((0, mxf), (1, mxf2)):
                    p.dma("sync", cx.scA[:, :],
                          fcscr3[ss][:, :, ot * 128:(ot + 1) * 128])
                    p.em("vector", lambda e, mx=mx: e.tensor_copy(out=mx, in_=cx.scA[:, :]))
                for ss, mx, psp in ((0, mxf, cx.psA), (1, mxf2, cx.psB)):
                    for jt in range(NC):
                        p.em("tensor", lambda e, ss=ss, jt=jt, mx=mx, psp=psp: e.matmul(
                            psp[:, :], mx[:, jt * 128:(jt + 1) * 128],
                            u1fc[ss][:, jt * 512:(jt + 1) * 512],
                            start=(jt == 0), stop=(jt == NC - 1)))
                p.em("scalar", lambda e: e.copy(out=m0b[:, :], in_=cx.psA[:, :]))
                p.em("scalar", lambda e: e.copy(out=m1b[:, :], in_=cx.psB[:, :]))
                for gloc in range(2):
                    g2i = 2 * ot + gloc
                    for si, (s2s, mb) in enumerate(((s2f[0], m0b), (s2f[1], m1b))):
                        p.em("tensor", lambda e, gloc=gloc, g2i=g2i, s2s=s2s, mb=mb, si=si: e.matmul(
                            cx.psO[gloc * 64:(gloc + 1) * 64, :],
                            s2s[gloc * 64:(gloc + 1) * 64, g2i * 64:(g2i + 1) * 64],
                            mb[gloc * 64:(gloc + 1) * 64, :],
                            start=(si == 0), stop=(si == 1)))
                for gloc in range(2):
                    p.em("scalar", lambda e, gloc=gloc: e.activation(
                        cx.scA[0:64, 0:512], cx.psO[gloc * 64:(gloc + 1) * 64, :], AF.Relu))
                    p.em("vector", lambda e, gloc=gloc, ot=ot: e.tensor_tensor(
                        out=h2T[gloc * 64:(gloc + 1) * 64, ot * 512:(ot + 1) * 512],
                        in0=cx.scA[0:64, 0:512], in1=cx.scA[0:64, 0:512], op=ALU.mult))
            for t4 in range(4):
                gi = tc * 4 + t4
                p.dma("gpsimd", otile[t4][:, :], yscr[gi * 128:(gi + 1) * 128, :])
            for ss in range(2):
                for g in range(64):
                    gb = (g % 2) * 64
                    p.em("tensor", lambda e, ss=ss, g=g, gb=gb: e.matmul(
                        cx.psO[0:64, :], s1p[ss][gb:gb + 64, g * 64:(g + 1) * 64],
                        h2T[gb:gb + 64, (g // 2) * 512:(g // 2 + 1) * 512],
                        start=True, stop=True))
                    p.em("scalar", lambda e, g=g, gb=gb: e.copy(
                        out=u1pj[gb:gb + 64, (g // 2) * 512:(g // 2 + 1) * 512],
                        in_=cx.psO[0:64, :]))
                for ot in range(8):
                    for q in range(4):
                        p.dma("sync", cx.scA[:, :],
                              pjscr3[ss][:, q * 8:(q + 1) * 8, ot * 128:(ot + 1) * 128])
                        p.em("vector", lambda e, q=q: e.tensor_copy(
                            out=mxp[:, q * 1024:(q + 1) * 1024], in_=cx.scA[:, :]))
                    for jt in range(32):
                        p.em("tensor", lambda e, jt=jt: e.matmul(
                            cx.psA[:, :], mxp[:, jt * 128:(jt + 1) * 128],
                            u1pj[:, jt * 512:(jt + 1) * 512],
                            start=(jt == 0), stop=(jt == 31)))
                    p.em("scalar", lambda e: e.copy(out=mpj[:, :], in_=cx.psA[:, :]))
                    for t4 in range(4):
                        for gloc in range(2):
                            g2i = 2 * ot + gloc
                            p.em("tensor", lambda e, t4=t4, gloc=gloc, g2i=g2i, ss=ss: e.matmul(
                                cx.psB[:, 0:64], mpj[gloc * 64:(gloc + 1) * 64, t4 * 128:(t4 + 1) * 128],
                                s2p[ss][gloc * 64:(gloc + 1) * 64, g2i * 64:(g2i + 1) * 64],
                                start=True, stop=True))
                            p.em("vector", lambda e, t4=t4, g2i=g2i: e.tensor_add(
                                otile[t4][:, g2i * 64:(g2i + 1) * 64],
                                otile[t4][:, g2i * 64:(g2i + 1) * 64], cx.psB[:, 0:64]))
            for t4 in range(4):
                gi = tc * 4 + t4
                p.dma("gpsimd", cx.xtile[:, :], xin[gi * 128:(gi + 1) * 128, :])
                p.em("vector", lambda e, t4=t4: e.scalar_tensor_tensor(
                    out=cx.xtile[:, :], in0=otile[t4][:, :], scalar=0.5, in1=cx.xtile[:, :],
                    op0=ALU.mult, op1=ALU.add))
                p.dma("gpsimd", xout[gi * 128:(gi + 1) * 128, :], cx.xtile[:, :])
        esB.close()



def build_l0ffn():
    nc = bass.Bass("TRN2", target_bir_lowering=False)
    p = Prog()
    es = ExitStack()
    with es:
        cx = Ctx(nc, p, es)
        cx.xin = cx.inp("xin", [T, D])
        cx.xout = nc.dram_tensor("xout", [T, D], F32, kind="ExternalOutput").ap()
        cx.setup_common()
        cx.cur = es.enter_context(ExitStack())
        emit_l0ffn(cx)
        sem = es.enter_context(nc.semaphore("tick"))
        semP = es.enter_context(nc.semaphore("pref"))
        block = es.enter_context(nc.Block())
        p.finalize(nc, block, {"T": sem, "P": semP})
    return nc


def _mixT2(mix, nin):
    B, o, j = mix.shape
    blk = 64
    bm = mix.reshape(B, o, j // (blk * nin) if False else nin, blk) if False else None
    m = mix.reshape(B, o, blk, nin)
    bm = np.ascontiguousarray(m.transpose(0, 3, 2, 1)).reshape(B, nin * blk, o)
    return np.ascontiguousarray(np.concatenate([bm, bm], axis=1))


def l0ffn_inputs(full, x):
    c = host_consts()
    return {"xin": x,
            "wqT": np.ascontiguousarray(full["L0_ffn_wq"].T),
            "wkT": np.ascontiguousarray(full["L0_ffn_wk"].T),
            "wvT": np.ascontiguousarray(full["L0_ffn_wv"].T),
            "woT": np.ascontiguousarray(full["L0_ffn_wo"].T),
            "qg16r": np.ascontiguousarray(full["L0_ffn_q_gain"][None, :]),
            "kg16r": np.ascontiguousarray(np.repeat(full["L0_ffn_k_gain"], 4)[None, :]),
            "kg4r": np.ascontiguousarray(full["L0_ffn_k_gain"][None, :]),
            "fc_s1": full["L0_ffn_fc_s1"], "fc_s2": full["L0_ffn_fc_s2"],
            "fcmixT2": _mixT2(full["L0_ffn_fc_mix"], 16),
            "fc_lg": np.ascontiguousarray(full["L0_ffn_fc_logits"][None, :]),
            "pj_s1": full["L0_ffn_pj_s1"], "pj_s2": full["L0_ffn_pj_s2"],
            "pjmixT2": _mixT2(full["L0_ffn_pj_mix"], 64),
            "pj_lg": np.ascontiguousarray(full["L0_ffn_pj_logits"][None, :]),
            "r0c": c["r0c"], "r0s": c["r0s"],
            "iota8f": np.arange(8, dtype=np.float32)[None, :],
            "identF": c["ident"], "m0tF": c["m0t"], "triF": c["tri01"]}

_PROGS = {}


def _get_prog(name):
    if name not in _PROGS:
        _PROGS[name] = {"l0ffn": build_l0ffn, "l0attn": build_l0attn,
                        "l1ffn": build_l1ffn, "l1attn": build_l1attn}[name]()
    return _PROGS[name]


def kernel(**inputs):
    if FUSED:
        return kernel_fused(inputs)
    full = inputs
    xs = [np.ascontiguousarray(inputs["L0_attn_x"][i]) for i in range(8)]
    seq = [("l0ffn", l0ffn_inputs), ("l0attn", l0attn_inputs), ("l0ffn", l0ffn_inputs),
           ("l1ffn", l1ffn_inputs), ("l1attn", l1attn_inputs), ("l1ffn", l1ffn_inputs)]
    wmaps = {}
    for name, fn in seq:
        if name not in wmaps:
            m = fn(full, xs[0])
            del m["xin"]
            wmaps[name] = m
    for name, _fn in seq:
        nc = _get_prog(name)
        in_maps = [dict(wmaps[name], xin=xs[i]) for i in range(8)]
        res = run_bass_kernel_spmd(nc, in_maps, core_ids=list(range(8)))
        xs = [np.asarray(res.results[i]["xout"]) for i in range(8)]
    return np.stack(xs, axis=0)

def build_fused():
    nc = bass.Bass("TRN2", target_bir_lowering=False)
    p = Prog()
    es = ExitStack()
    with es:
        cx = Ctx(nc, p, es)
        xin = nc.dram_tensor("xin", [T, D], F32, kind="ExternalInput").ap()
        xout = nc.dram_tensor("xout", [T, D], F32, kind="ExternalOutput").ap()
        xA = nc.dram_tensor("xA", [T, D], F32, kind="Internal").ap()
        xB = nc.dram_tensor("xB", [T, D], F32, kind="Internal").ap()
        cx.setup_common()
        stages = [(emit_l0ffn, "F_", xin, xA), (emit_l0attn, "A_", xA, xB),
                  (emit_l0ffn, "F_", xB, xA), (emit_l1ffn, "M_", xA, xB),
                  (emit_l1attn, "G_", xB, xA), (emit_l1ffn, "M_", xA, xout)]
        for fn, pref, xi, xo in stages:
            cx.prefix = pref
            cx.xin, cx.xout = xi, xo
            cx.cur = ExitStack()
            with cx.cur:
                fn(cx)
        cx.cur = es
        sem = es.enter_context(nc.semaphore("tick"))
        semP = es.enter_context(nc.semaphore("pref"))
        block = es.enter_context(nc.Block())
        p.finalize(nc, block, {"T": sem, "P": semP})
    return nc


FUSED = True


def _prefixed(fn, full, pref):
    m = fn(full, None)
    for k in ("xin", "identF", "m0tF", "triF"):
        m.pop(k, None)
    return {pref + k: v for k, v in m.items()}


def kernel_fused(inputs):
    xfull = inputs["L0_attn_x"]
    nc = _PROGS.get("fused")
    if nc is None:
        nc = _PROGS["fused"] = build_fused()
    c = host_consts()
    wmap = {"identF": c["ident"], "m0tF": c["m0t"], "triF": c["tri01"]}
    wmap.update(_prefixed(l0ffn_inputs, inputs, "F_"))
    wmap.update(_prefixed(l0attn_inputs, inputs, "A_"))
    wmap.update(_prefixed(l1ffn_inputs, inputs, "M_"))
    wmap.update(_prefixed(l1attn_inputs, inputs, "G_"))
    in_maps = [dict(wmap, xin=np.ascontiguousarray(xfull[i])) for i in range(8)]
    res = run_bass_kernel_spmd(nc, in_maps, core_ids=list(range(8)))
    return np.stack([np.asarray(res.results[i]["xout"]) for i in range(8)], axis=0)


# version 08 (final)
# speedup vs baseline: 1.3605x; 1.3605x over previous
import numpy as np
import concourse.bass as bass
import concourse.mybir as mybir
from concourse.bass_utils import run_bass_kernel_spmd
from contextlib import ExitStack

F32 = mybir.dt.float32
BF16 = mybir.dt.bfloat16
I32 = mybir.dt.int32
AF = mybir.ActivationFunctionType
ALU = mybir.AluOpType
AX = mybir.AxisListType

T, D = 2048, 1024
NT = T // 128
NC = D // 128
NEG = -1e30


class Prog:
    def __init__(self):
        self.steps = []

    def em(self, eng, fn, inc=1):
        self.steps.append((eng, fn, inc))

    def dma(self, eng, out, in_):
        self.em(eng, lambda e, o=out, i=in_: e.dma_start(out=o, in_=i), inc=16)

    def finalize(self, nc, block, sem):
        waits, t = [], 0
        for (_, _, inc) in self.steps:
            waits.append(t)
            t += inc
        total = t
        SKIP_OK = ("tensor",)
        def group(step):
            en, _, inc = step
            return en + "_dma" if inc == 16 else en
        prev_idx = {}
        need_wait = []
        for i, st in enumerate(self.steps):
            g = group(st)
            pi = prev_idx.get(g)
            if pi is None or g not in SKIP_OK:
                need_wait.append(waits[i] > 0)
            else:
                gap = any(group(self.steps[j]) != g for j in range(pi + 1, i))
                need_wait.append(gap)
            prev_idx[g] = i
        for eng_name in ["sync", "scalar", "vector", "tensor", "gpsimd"]:
            def body(eng, eng_name=eng_name):
                for (en, fn, inc), w, nw in zip(self.steps, waits, need_wait):
                    if en != eng_name:
                        continue
                    if nw and w > 0:
                        eng.wait_ge(sem, w)
                    fn(eng).then_inc(sem, inc)
                eng.wait_ge(sem, total)
            getattr(block, eng_name)(body)
        return total


_CONSTS = None


def host_consts():
    global _CONSTS
    if _CONSTS is not None:
        return _CONSTS
    ident = np.eye(128, dtype=np.float32)
    jj = np.arange(128)[None, :]
    pp = np.arange(128)[:, None]
    m0t = np.where(jj >= pp, 0.0, NEG).astype(np.float32)
    tri01 = (jj >= pp).astype(np.float32)
    t = np.arange(T, dtype=np.float64)
    inv = 1.0 / (10000.0 ** (np.arange(0, 64, 2, dtype=np.float64) / 64))
    fr = np.outer(t, inv)
    c = np.cos(fr); s = np.sin(fr)
    r0c = np.tile(np.concatenate([c, c], 1), (1, 16)).astype(np.float32)
    r0s = np.tile(np.concatenate([s, -s], 1), (1, 16)).astype(np.float32)
    inv2 = 10000.0 ** (-2.0 * np.arange(32, dtype=np.float64) / 64)
    ang = np.outer(t, inv2)
    c2 = np.repeat(np.cos(ang), 2, axis=1); s2 = np.repeat(np.sin(ang), 2, axis=1)
    r1c = np.tile(c2, (1, 16)).astype(np.float32)
    r1s = np.tile(s2, (1, 16)).astype(np.float32)
    _CONSTS = dict(ident=ident, m0t=m0t, tri01=tri01, r0c=r0c, r0s=r0s, r1c=r1c, r1s=r1s)
    return _CONSTS


class Ctx:
    def __init__(self, nc, p, es):
        self.nc, self.p, self.es = nc, p, es
        self.cur = es
        self.prefix = ""
        self._inps = {}
        self._internals = {}
        self._nsb = 0

    def sb(self, name, shape, dtype):
        self._nsb += 1
        return self.cur.enter_context(self.nc.sbuf_tensor(f"{name}_{self._nsb}", shape, dtype))

    def mk(self, stack):
        def alloc(n, sh, dt):
            self._nsb += 1
            return stack.enter_context(self.nc.sbuf_tensor(f"{n}_{self._nsb}", sh, dt))
        return alloc

    def ps(self, name, shape, dtype=F32):
        return self.es.enter_context(self.nc.psum_tensor(name, shape, dtype))

    def inp(self, name, shape, dtype=F32):
        nm = self.prefix + name
        if nm not in self._inps:
            self._inps[nm] = self.nc.dram_tensor(nm, shape, dtype, kind="ExternalInput").ap()
        return self._inps[nm]

    def internal(self, name, shape, dtype=F32):
        nm = self.prefix + name
        if nm not in self._internals:
            self._internals[nm] = self.nc.dram_tensor(nm, shape, dtype, kind="Internal").ap()
        return self._internals[nm]

    def setup_common(self):
        p = self.p
        self.identF = self.inp("identF", [128, 128])
        self.m0tF = self.inp("m0tF", [128, 128])
        self.triF = self.inp("triF", [128, 128])
        self.sIdent = self.sb("c_ident", [128, 128], F32)
        self.sIdentB = self.sb("c_identb", [128, 128], BF16)
        self.sM0 = self.sb("c_m0", [128, 128], F32)
        self.sTri = self.sb("c_tri", [128, 128], F32)
        self.sOnes = self.sb("c_ones", [1, 128], F32)
        p.dma("sync", self.sIdent[:, :], self.identF)
        p.dma("sync", self.sM0[:, :], self.m0tF)
        p.dma("sync", self.sTri[:, :], self.triF)
        p.em("vector", lambda e: e.tensor_copy(out=self.sIdentB[:, :], in_=self.sIdent[:, :]))
        p.em("vector", lambda e: e.memset(self.sOnes[:, :], 1.0))
        self.scA = self.sb("scrapA", [128, 1024], F32)
        self.scB = self.sb("scrapB", [128, 1024], F32)
        self.bB = self.sb("scrapBF", [128, 1024], BF16)
        self.xtile = self.sb("xtile", [128, 1024], F32)
        self.st = self.sb("stats", [128, 64], F32)
        self.psA = self.ps("psA", [128, 512])
        self.psB = self.ps("psB", [128, 512])
        self.psO = self.ps("psO", [128, 512])
        self.psT = self.ps("psT", [128, 128])
        self.psTB = self.ps("psTB", [128, 128], BF16)

    def tr(self, src_ap, dst_ap, bf16=True, np_=128, nf=128):
        p = self.p
        ident = self.sIdentB if bf16 else self.sIdent
        pst = self.psTB if bf16 else self.psT
        p.em("tensor", lambda e: e.transpose(pst[0:nf, 0:np_], src_ap, ident[0:np_, 0:np_]))
        p.em("scalar", lambda e: e.copy(out=dst_ap, in_=pst[0:nf, 0:np_]))

    def bcast_row(self, row_ap, dst_ap, nparts, ncols):
        p = self.p
        p.em("tensor", lambda e: e.matmul(self.psT[0:nparts, 0:ncols], self.sOnes[0:1, 0:nparts],
                                          row_ap, start=True, stop=True))
        p.em("scalar", lambda e: e.copy(out=dst_ap, in_=self.psT[0:nparts, 0:ncols]))

    def norm_tile(self, x_ap, z32_ap, zb_ap, rms, eps):
        p, st, scA, scB = self.p, self.st, self.scA, self.scB
        n = 1024.0
        src = x_ap
        if not rms:
            p.em("vector", lambda e: e.reduce_sum(st[:, 0:1], x_ap, axis=AX.X))
            p.em("vector", lambda e: e.tensor_scalar(out=st[:, 0:1], in0=st[:, 0:1],
                                                     scalar1=-1.0 / n, scalar2=None, op0=ALU.mult))
            p.em("vector", lambda e: e.tensor_scalar_add(scA[:, :], x_ap, st[:, 0:1]))
            src = scA[:, :]
        p.em("scalar", lambda e, src=src: e.activation(scB[:, :], src, AF.Square,
                                                       accum_out=st[:, 1:2]))
        p.em("vector", lambda e: e.tensor_scalar(out=st[:, 1:2], in0=st[:, 1:2],
                                                 scalar1=1.0 / n, scalar2=eps, op0=ALU.mult, op1=ALU.add))
        p.em("scalar", lambda e: e.sqrt(st[:, 2:3], st[:, 1:2]))
        p.em("vector", lambda e: e.reciprocal(st[:, 3:4], st[:, 2:3]))
        if z32_ap is not None:
            p.em("vector", lambda e, src=src: e.tensor_scalar_mul(z32_ap, src, st[:, 3:4]))
        if zb_ap is not None:
            p.em("vector", lambda e, src=src: e.tensor_scalar_mul(zb_ap, src, st[:, 3:4]))


def load_w_bf16(cx, hbm_ap, slab, rows, cols):
    p = cx.p
    for j in range(rows // 128):
        p.dma("sync", cx.scA[:, 0:cols], hbm_ap[j * 128:(j + 1) * 128, :])
        p.em("vector", lambda e, j=j: e.tensor_copy(out=slab[:, j * cols:(j + 1) * cols],
                                                    in_=cx.scA[:, 0:cols]))


def emit_ln_zT(cx, xin, zT, rms=False, z32slab=None, zbslab=None):
    p = cx.p
    for i in range(NT):
        p.dma("sync", cx.xtile[:, :], xin[i * 128:(i + 1) * 128, :])
        z32 = z32slab[:, i * 1024:(i + 1) * 1024] if z32slab is not None else None
        zb = zbslab[:, i * 1024:(i + 1) * 1024] if zbslab is not None else cx.bB[:, :]
        cx.norm_tile(cx.xtile[:, :], z32, zb, rms, 1e-5)
        if zT is not None:
            for j in range(NC):
                cx.tr(zb[:, j * 128:(j + 1) * 128], zT[j][:, i * 128:(i + 1) * 128], bf16=True)


def emit_l0attn(cx):
    nc, p, es = cx.nc, cx.p, cx.cur
    if True:
        xin = cx.xin
        wqT = cx.inp("wqT", [D, D])
        wkT = cx.inp("wkT", [D, D])
        wvT = cx.inp("wvT", [D, D])
        r1c = cx.inp("r1c", [T, D])
        r1s = cx.inp("r1s", [T, D])
        xout = cx.xout

        qT = [cx.sb(f"qT{j}", [128, T], BF16) for j in range(NC)]
        kT = [cx.sb(f"kT{j}", [128, T], BF16) for j in range(NC)]
        vext = cx.sb("vext", [128, NT * 16 * 65], BF16)
        qmax = cx.sb("qmax", [128, 16], F32)
        kmax = cx.sb("kmax", [128, 16], F32)
        es1 = es.enter_context(ExitStack())
        sb1 = cx.mk(es1)
        zT = [sb1(f"zT{j}", [128, T], BF16) for j in range(NC)]
        emit_ln_zT(cx, xin, zT, rms=False)
        wslab = sb1("wslab", [128, NC * 1024], BF16)
        ctile = sb1("ctile", [128, 1024], F32)
        stile = sb1("stile", [128, 1024], F32)
        p.em("vector", lambda e: e.memset(qmax[:, :], 0.0))
        p.em("vector", lambda e: e.memset(kmax[:, :], 0.0))

        def proj_rope(outT, maxacc):
            for i in range(NT):
                p.dma("sync", ctile[:, :], r1c[i * 128:(i + 1) * 128, :])
                p.dma("sync", stile[:, :], r1s[i * 128:(i + 1) * 128, :])
                for ch in range(2):
                    psp = cx.psA if ch == 0 else cx.psB
                    for j in range(NC):
                        p.em("tensor", lambda e, i=i, j=j, ch=ch, psp=psp:
                             e.matmul(psp[:, :], zT[j][:, i * 128:(i + 1) * 128],
                                      wslab[:, j * 1024 + ch * 512: j * 1024 + (ch + 1) * 512],
                                      start=(j == 0), stop=(j == NC - 1)))
                    p.em("scalar", lambda e, ch=ch, psp=psp:
                         e.copy(out=cx.scA[:, ch * 512:(ch + 1) * 512], in_=psp[:, :]))
                q2 = cx.scA[:, :].rearrange("p (a two) -> p a two", two=2)
                r2 = cx.scB[:, :].rearrange("p (a two) -> p a two", two=2)
                p.em("vector", lambda e: e.tensor_scalar_mul(r2[:, :, 0], q2[:, :, 1], -1.0))
                p.em("vector", lambda e: e.tensor_copy(out=r2[:, :, 1], in_=q2[:, :, 0]))
                p.em("vector", lambda e: e.tensor_mul(cx.scA[:, :], cx.scA[:, :], ctile[:, :]))
                p.em("vector", lambda e: e.tensor_mul(cx.scB[:, :], cx.scB[:, :], stile[:, :]))
                p.em("vector", lambda e: e.tensor_add(cx.scA[:, :], cx.scA[:, :], cx.scB[:, :]))
                p.em("scalar", lambda e: e.activation(cx.scB[:, :], cx.scA[:, :], AF.Square))
                p.em("vector", lambda e: e.reduce_sum(
                    cx.st[:, 8:24], cx.scB[:, :].rearrange("p (h d) -> p h d", h=16), axis=AX.X))
                p.em("vector", lambda e, m=maxacc: e.tensor_max(m[:, :], m[:, :], cx.st[:, 8:24]))
                p.em("vector", lambda e: e.tensor_copy(out=cx.bB[:, :], in_=cx.scA[:, :]))
                for j in range(NC):
                    cx.tr(cx.bB[:, j * 128:(j + 1) * 128], outT[j][:, i * 128:(i + 1) * 128])

        load_w_bf16(cx, wqT, wslab, D, 1024)
        proj_rope(qT, qmax)
        load_w_bf16(cx, wkT, wslab, D, 1024)
        proj_rope(kT, kmax)
        load_w_bf16(cx, wvT, wslab, D, 1024)
        for i in range(NT):
            base = i * 16 * 65
            for ch in range(2):
                psp = cx.psA if ch == 0 else cx.psB
                for j in range(NC):
                    p.em("tensor", lambda e, i=i, j=j, psp=psp, ch=ch:
                         e.matmul(psp[:, :], zT[j][:, i * 128:(i + 1) * 128],
                                  wslab[:, j * 1024 + ch * 512: j * 1024 + (ch + 1) * 512],
                                  start=(j == 0), stop=(j == NC - 1)))
                dst = vext[:, base + ch * 8 * 65: base + (ch + 1) * 8 * 65].rearrange(
                    "p (h w) -> p h w", w=65)
                p.em("vector", lambda e, psp=psp, dst=dst:
                     e.tensor_copy(out=dst[:, :, 0:64], in_=psp[:, :].rearrange("p (h d) -> p h d", d=64)))
            oc = vext[:, base: base + 16 * 65].rearrange("p (h w) -> p h w", w=65)[:, :, 64]
            p.em("vector", lambda e, oc=oc: e.memset(oc, 1.0))

        es1.close()
        mrow = cx.sb("mrow", [1, 16], F32)
        mneg = cx.sb("mneg", [128, 16], F32)
        tmax = cx.sb("tmax", [16, 128], F32)
        for col, acc in ((33, qmax), (34, kmax)):
            cx.tr(acc[:, :], tmax[0:16, 0:128], bf16=False, np_=128, nf=16)
            p.em("vector", lambda e: e.reduce_max(cx.st[0:16, 32:33], tmax[0:16, 0:128], axis=AX.X))
            p.em("scalar", lambda e, col=col: e.sqrt(cx.st[0:16, col:col + 1], cx.st[0:16, 32:33]))
        p.em("vector", lambda e: e.tensor_mul(cx.st[0:16, 35:36], cx.st[0:16, 33:34], cx.st[0:16, 34:35]))
        p.em("vector", lambda e: e.tensor_scalar_mul(cx.st[0:16, 35:36], cx.st[0:16, 35:36], -0.125))
        cx.tr(cx.st[0:16, 35:36], mrow[0:1, 0:16], bf16=False, np_=16, nf=1)
        cx.bcast_row(mrow[0:1, 0:16], mneg[:, :], 128, 16)

        attT = [cx.sb(f"attT{j}", [128, T], BF16) for j in range(NC)]
        pexp = cx.sb("pexp", [128, NT * 512], BF16)
        rrow = cx.sb("rrow", [1, 512], F32)
        rb = cx.sb("rb", [64, 512], F32)
        for h in range(16):
            jt, off = h // 2, (h % 2) * 64
            for c in range(4):
                for kt in range(NT):
                    p.em("tensor", lambda e, c=c, kt=kt, jt=jt, off=off:
                         e.matmul(cx.psA[:, :], kT[jt][off:off + 64, kt * 128:(kt + 1) * 128],
                                  qT[jt][off:off + 64, c * 512:(c + 1) * 512],
                                  start=True, stop=True))
                    p.em("scalar", lambda e, kt=kt, h=h:
                         e.activation(pexp[:, kt * 512:(kt + 1) * 512], cx.psA[:, :], AF.Exp,
                                      bias=mneg[:, h:h + 1], scale=0.125))
                for kt in range(NT):
                    p.em("tensor", lambda e, kt=kt, h=h:
                         e.matmul(cx.psO[0:65, :], vext[:, kt * 16 * 65 + h * 65: kt * 16 * 65 + (h + 1) * 65],
                                  pexp[:, kt * 512:(kt + 1) * 512],
                                  start=(kt == 0), stop=(kt == NT - 1)))
                p.em("vector", lambda e: e.reciprocal(rrow[0:1, :], cx.psO[64:65, :]))
                p.em("tensor", lambda e: e.matmul(cx.psB[0:64, :], cx.sOnes[0:1, 0:64], rrow[0:1, :],
                                                  start=True, stop=True))
                p.em("scalar", lambda e: e.copy(out=rb[:, :], in_=cx.psB[0:64, :]))
                p.em("vector", lambda e, c=c, jt=jt, off=off:
                     e.tensor_mul(attT[jt][off:off + 64, c * 512:(c + 1) * 512],
                                  cx.psO[0:64, :], rb[:, :]))
        for i in range(NT):
            p.dma("sync", cx.xtile[:, :], xin[i * 128:(i + 1) * 128, :])
            for j in range(NC):
                p.em("tensor", lambda e, i=i, j=j: e.transpose(
                    cx.psTB[:, :], attT[j][:, i * 128:(i + 1) * 128], cx.sIdentB[:, :]))
                p.em("vector", lambda e, j=j: e.tensor_add(
                    cx.xtile[:, j * 128:(j + 1) * 128], cx.xtile[:, j * 128:(j + 1) * 128],
                    cx.psTB[:, :]))
            p.dma("sync", xout[i * 128:(i + 1) * 128, :], cx.xtile[:, :])



def build_l0attn():
    nc = bass.Bass("TRN2", target_bir_lowering=False)
    p = Prog()
    es = ExitStack()
    with es:
        cx = Ctx(nc, p, es)
        cx.xin = cx.inp("xin", [T, D])
        cx.xout = nc.dram_tensor("xout", [T, D], F32, kind="ExternalOutput").ap()
        cx.setup_common()
        cx.cur = es.enter_context(ExitStack())
        emit_l0attn(cx)
        sem = es.enter_context(nc.semaphore("tick"))
        block = es.enter_context(nc.Block())
        p.finalize(nc, block, sem)
    return nc


def l0attn_inputs(full, x):
    c = host_consts()
    return {"xin": x,
            "wqT": np.ascontiguousarray(full["L0_attn_Wq"].T),
            "wkT": np.ascontiguousarray(full["L0_attn_Wk"].T),
            "wvT": np.ascontiguousarray(full["L0_attn_Wv"].T),
            "r1c": c["r1c"], "r1s": c["r1s"],
            "identF": c["ident"], "m0tF": c["m0t"], "triF": c["tri01"]}



def emit_l1ffn(cx):
    nc, p, es = cx.nc, cx.p, cx.cur
    if True:
        xin = cx.xin
        gwT = cx.inp("gwT", [D, 32])
        ebias = cx.inp("ebias", [1, 32])
        guT = cx.inp("guT", [32, D, D])
        dnT = cx.inp("dnT", [32, 512, D])
        sgT = cx.inp("sgT", [D, D])
        suT = cx.inp("suT", [D, D])
        sdT = cx.inp("sdT", [D, D])
        xout = cx.xout

        zT = [cx.sb(f"zT{j}", [128, T], BF16) for j in range(NC)]
        combine = cx.sb("combine", [128, NT * 32], F32)
        ACC = cx.sb("ACC", [128, NT * 1024], F32)
        for i in range(NT):
            p.em("vector", lambda e, i=i: e.memset(ACC[:, i * 1024:(i + 1) * 1024], 0.0))

        es1 = es.enter_context(ExitStack())
        sb1 = cx.mk(es1)
        z32t = sb1("z32t", [128, 1024], F32)
        zt32tmp = sb1("zt32tmp", [128, 1024], F32)
        gw32 = sb1("gw32", [128, NC * 32], F32)
        biasb = sb1("biasb", [128, 32], F32)
        sc = sb1("sc", [128, 32], F32)
        sfc = sb1("sfc", [128, 32], F32)
        gs = sb1("gs", [128, 8], F32)
        rkg = sb1("rkg", [128, 8], F32)
        gm = sb1("gm", [128, 8], F32)
        sm = sb1("sm", [128, 32], F32)
        msk = sb1("msk", [128, 32], F32)
        rke = sb1("rke", [128, 32], F32)
        cmp32 = sb1("cmp32", [128, 32], F32)
        for j in range(NC):
            p.dma("sync", cx.scA[:, 0:32], gwT[j * 128:(j + 1) * 128, :])
            p.em("vector", lambda e, j=j: e.tensor_copy(out=gw32[:, j * 32:(j + 1) * 32],
                                                        in_=cx.scA[:, 0:32]))
        p.dma("sync", cx.scA[0:1, 0:32], ebias)
        cx.bcast_row(cx.scA[0:1, 0:32], biasb[:, :], 128, 32)
        for i in range(NT):
            p.dma("sync", cx.xtile[:, :], xin[i * 128:(i + 1) * 128, :])
            cx.norm_tile(cx.xtile[:, :], z32t[:, :], cx.bB[:, :], False, 1e-5)
            for j in range(NC):
                cx.tr(cx.bB[:, j * 128:(j + 1) * 128], zT[j][:, i * 128:(i + 1) * 128])
            for j in range(NC):
                p.em("tensor", lambda e, j=j: e.transpose(cx.psT[:, :],
                                                          z32t[:, j * 128:(j + 1) * 128],
                                                          cx.sIdent[:, :]))
                p.em("scalar", lambda e, j=j: e.copy(out=zt32tmp[:, j * 128:(j + 1) * 128],
                                                     in_=cx.psT[:, :]))
            for j in range(NC):
                p.em("tensor", lambda e, j=j: e.matmul(cx.psB[:, 0:32],
                                                       zt32tmp[:, j * 128:(j + 1) * 128],
                                                       gw32[:, j * 32:(j + 1) * 32],
                                                       start=(j == 0), stop=(j == NC - 1)))
            p.em("scalar", lambda e: e.activation(sc[:, :], cx.psB[:, 0:32], AF.Sigmoid))
            p.em("vector", lambda e: e.tensor_add(sfc[:, :], sc[:, :], biasb[:, :]))
            p.em("vector", lambda e: e.reduce_max(gs[:, :], sfc[:, :].rearrange("p (g f) -> p g f", g=8),
                                                  axis=AX.X))
            for g in range(8):
                p.em("vector", lambda e, g=g: e.tensor_single_scalar(
                    out=cx.st[:, 8:16], in_=gs[:, :], scalar=gs[:, g:g + 1], op=ALU.is_gt))
                p.em("vector", lambda e, g=g: e.reduce_sum(rkg[:, g:g + 1], cx.st[:, 8:16], axis=AX.X))
            p.em("vector", lambda e: e.tensor_single_scalar(out=gm[:, :], in_=rkg[:, :],
                                                            scalar=3.5, op=ALU.is_lt))
            for ii in range(4):
                p.em("vector", lambda e, ii=ii: e.tensor_copy(
                    out=sm[:, :].rearrange("p (g f) -> p g f", g=8)[:, :, ii], in_=gm[:, :]))
            p.em("vector", lambda e: e.tensor_mul(msk[:, :], sfc[:, :], sm[:, :]))
            for ei in range(32):
                p.em("vector", lambda e, ei=ei: e.tensor_single_scalar(
                    out=cmp32[:, :], in_=msk[:, :], scalar=msk[:, ei:ei + 1], op=ALU.is_gt))
                p.em("vector", lambda e, ei=ei: e.reduce_sum(rke[:, ei:ei + 1], cmp32[:, :], axis=AX.X))
            p.em("vector", lambda e: e.tensor_single_scalar(out=cmp32[:, :], in_=rke[:, :],
                                                            scalar=5.5, op=ALU.is_lt))
            p.em("vector", lambda e: e.tensor_mul(cmp32[:, :], cmp32[:, :], sc[:, :]))
            p.em("vector", lambda e: e.reduce_sum(cx.st[:, 4:5], cmp32[:, :], axis=AX.X))
            p.em("vector", lambda e: e.tensor_scalar_add(cx.st[:, 4:5], cx.st[:, 4:5], 1e-20))
            p.em("vector", lambda e: e.reciprocal(cx.st[:, 5:6], cx.st[:, 4:5]))
            p.em("vector", lambda e, i=i: e.tensor_scalar(
                out=combine[:, i * 32:(i + 1) * 32], in0=cmp32[:, :],
                scalar1=cx.st[:, 5:6], scalar2=2.5, op0=ALU.mult, op1=ALU.mult))
        es1.close()

        es2 = es.enter_context(ExitStack())
        sb2 = cx.mk(es2)
        wg = sb2("wg", [128, NC * 1024], BF16)
        wu = sb2("wu", [128, NC * 1024], BF16)
        ATS = sb2("ATS", [128, NC * 2048], BF16)
        load_w_bf16(cx, sgT, wg, D, 1024)
        load_w_bf16(cx, suT, wu, D, 1024)
        for tc in range(4):
            for og in range(8):
                for j in range(NC):
                    p.em("tensor", lambda e, tc=tc, og=og, j=j: e.matmul(
                        cx.psA[:, :], wg[:, j * 1024 + og * 128: j * 1024 + (og + 1) * 128],
                        zT[j][:, tc * 512:(tc + 1) * 512], start=(j == 0), stop=(j == NC - 1)))
                for j in range(NC):
                    p.em("tensor", lambda e, tc=tc, og=og, j=j: e.matmul(
                        cx.psB[:, :], wu[:, j * 1024 + og * 128: j * 1024 + (og + 1) * 128],
                        zT[j][:, tc * 512:(tc + 1) * 512], start=(j == 0), stop=(j == NC - 1)))
                p.em("scalar", lambda e: e.activation(cx.scA[:, 0:512], cx.psA[:, :], AF.Silu))
                p.em("vector", lambda e, og=og, tc=tc: e.tensor_mul(
                    ATS[:, og * 2048 + tc * 512: og * 2048 + (tc + 1) * 512],
                    cx.psB[:, :], cx.scA[:, 0:512]))
        load_w_bf16(cx, sdT, wg, D, 1024)
        for i in range(NT):
            for ch in range(2):
                for j in range(NC):
                    p.em("tensor", lambda e, i=i, ch=ch, j=j: e.matmul(
                        cx.psA[:, :], ATS[:, j * 2048 + i * 128: j * 2048 + (i + 1) * 128],
                        wg[:, j * 1024 + ch * 512: j * 1024 + (ch + 1) * 512],
                        start=(j == 0), stop=(j == NC - 1)))
                p.em("vector", lambda e, i=i, ch=ch: e.tensor_add(
                    ACC[:, i * 1024 + ch * 512: i * 1024 + (ch + 1) * 512],
                    ACC[:, i * 1024 + ch * 512: i * 1024 + (ch + 1) * 512], cx.psA[:, :]))
        es2.close()

        es3 = es.enter_context(ExitStack())
        sb3 = cx.mk(es3)
        wge = sb3("wge", [128, NC * 1024], BF16)
        wd = sb3("wd", [128, 4 * 1024], BF16)
        AT = sb3("AT", [128, 4 * 2048], BF16)
        for ex in range(32):
            load_w_bf16(cx, guT[ex], wge, D, 1024)
            for j in range(4):
                p.dma("sync", cx.scA[:, :], dnT[ex][j * 128:(j + 1) * 128, :])
                p.em("vector", lambda e, j=j: e.tensor_copy(out=wd[:, j * 1024:(j + 1) * 1024],
                                                            in_=cx.scA[:, :]))
            for tc in range(4):
                for og in range(4):
                    for j in range(NC):
                        p.em("tensor", lambda e, og=og, tc=tc, j=j: e.matmul(
                            cx.psA[:, :], wge[:, j * 1024 + og * 128: j * 1024 + (og + 1) * 128],
                            zT[j][:, tc * 512:(tc + 1) * 512], start=(j == 0), stop=(j == NC - 1)))
                    for j in range(NC):
                        p.em("tensor", lambda e, og=og, tc=tc, j=j: e.matmul(
                            cx.psB[:, :], wge[:, j * 1024 + (og + 4) * 128: j * 1024 + (og + 5) * 128],
                            zT[j][:, tc * 512:(tc + 1) * 512], start=(j == 0), stop=(j == NC - 1)))
                    p.em("scalar", lambda e: e.activation(cx.scA[:, 0:512], cx.psA[:, :], AF.Silu))
                    p.em("vector", lambda e, og=og, tc=tc: e.tensor_mul(
                        AT[:, og * 2048 + tc * 512: og * 2048 + (tc + 1) * 512],
                        cx.psB[:, :], cx.scA[:, 0:512]))
            for i in range(NT):
                for ch in range(2):
                    for j in range(4):
                        p.em("tensor", lambda e, i=i, ch=ch, j=j: e.matmul(
                            cx.psA[:, :], AT[:, j * 2048 + i * 128: j * 2048 + (i + 1) * 128],
                            wd[:, j * 1024 + ch * 512: j * 1024 + (ch + 1) * 512],
                            start=(j == 0), stop=(j == 3)))
                    p.em("vector", lambda e, i=i, ch=ch, ex=ex: e.scalar_tensor_tensor(
                        out=ACC[:, i * 1024 + ch * 512: i * 1024 + (ch + 1) * 512],
                        in0=cx.psA[:, :], scalar=combine[:, i * 32 + ex: i * 32 + ex + 1],
                        in1=ACC[:, i * 1024 + ch * 512: i * 1024 + (ch + 1) * 512],
                        op0=ALU.mult, op1=ALU.add))
        es3.close()

        for i in range(NT):
            p.dma("sync", cx.xtile[:, :], xin[i * 128:(i + 1) * 128, :])
            p.em("vector", lambda e, i=i: e.scalar_tensor_tensor(
                out=cx.xtile[:, :], in0=ACC[:, i * 1024:(i + 1) * 1024], scalar=0.5,
                in1=cx.xtile[:, :], op0=ALU.mult, op1=ALU.add))
            p.dma("sync", xout[i * 128:(i + 1) * 128, :], cx.xtile[:, :])



def build_l1ffn():
    nc = bass.Bass("TRN2", target_bir_lowering=False)
    p = Prog()
    es = ExitStack()
    with es:
        cx = Ctx(nc, p, es)
        cx.xin = cx.inp("xin", [T, D])
        cx.xout = nc.dram_tensor("xout", [T, D], F32, kind="ExternalOutput").ap()
        cx.setup_common()
        cx.cur = es.enter_context(ExitStack())
        emit_l1ffn(cx)
        sem = es.enter_context(nc.semaphore("tick"))
        block = es.enter_context(nc.Block())
        p.finalize(nc, block, sem)
    return nc


def l1ffn_inputs(full, x):
    c = host_consts()
    return {"xin": x,
            "gwT": np.ascontiguousarray(full["L1_ffn_gate_weight"].T),
            "ebias": np.ascontiguousarray(full["L1_ffn_e_score_correction_bias"][None, :]),
            "guT": np.ascontiguousarray(full["L1_ffn_experts_gate_up"].transpose(0, 2, 1)),
            "dnT": np.ascontiguousarray(full["L1_ffn_experts_down"].transpose(0, 2, 1)),
            "sgT": np.ascontiguousarray(full["L1_ffn_shared_gate_w"].T),
            "suT": np.ascontiguousarray(full["L1_ffn_shared_up_w"].T),
            "sdT": np.ascontiguousarray(full["L1_ffn_shared_down_w"].T),
            "identF": c["ident"], "m0tF": c["m0t"], "triF": c["tri01"]}


def emit_l1attn(cx):
    nc, p, es = cx.nc, cx.p, cx.cur
    if True:
        xin = cx.xin
        Wq = cx.inp("Wq", [D, D]); Wk = cx.inp("Wk", [D, D]); Wv = cx.inp("Wv", [D, D])
        Wg1 = cx.inp("Wg1", [D, 16]); Wg2 = cx.inp("Wg2", [16, D])
        bg2r = cx.inp("bg2r", [1, D]); gnwr = cx.inp("gnwr", [1, D])
        Wg = cx.inp("Wg", [D, D]); Wo = cx.inp("Wo", [D, D])
        xout = cx.xout

        zT = [cx.sb(f"zT{j}", [128, T], BF16) for j in range(NC)]
        emit_ln_zT(cx, xin, zT, rms=False)

        wq = cx.sb("wq", [128, NC * 1024], BF16)
        wk = cx.sb("wk", [128, NC * 1024], BF16)
        wv = cx.sb("wv", [128, NC * 1024], BF16)
        wgt = cx.sb("wgt", [128, NC * 1024], BF16)
        wo = cx.sb("wo", [128, NC * 1024], BF16)
        for hbm, slab in ((Wq, wq), (Wk, wk), (Wv, wv), (Wg, wgt), (Wo, wo)):
            load_w_bf16(cx, hbm, slab, D, 1024)
        wg1 = cx.sb("wg1", [128, NC * 16], BF16)
        for j in range(NC):
            p.dma("sync", cx.scA[:, 0:16], Wg1[j * 128:(j + 1) * 128, :])
            p.em("vector", lambda e, j=j: e.tensor_copy(out=wg1[:, j * 16:(j + 1) * 16],
                                                        in_=cx.scA[:, 0:16]))
        wg2b = cx.sb("wg2b", [16, 1024], BF16)
        p.dma("sync", cx.scA[0:16, 0:1024], Wg2)
        p.em("vector", lambda e: e.tensor_copy(out=wg2b[:, :], in_=cx.scA[0:16, 0:1024]))
        bg2b = cx.sb("bg2b", [128, 1024], F32)
        gnwb = cx.sb("gnwb", [128, 1024], F32)
        rowt = cx.sb("rowt", [1, 1024], F32)
        for src, dst in ((bg2r, bg2b), (gnwr, gnwb)):
            p.dma("sync", rowt[:, :], src)
            for ch in range(2):
                p.em("tensor", lambda e, ch=ch: e.matmul(cx.psA[:, :], cx.sOnes[0:1, 0:128],
                                                         rowt[0:1, ch * 512:(ch + 1) * 512],
                                                         start=True, stop=True))
                p.em("scalar", lambda e, ch=ch, dst=dst: e.copy(out=dst[:, ch * 512:(ch + 1) * 512],
                                                                in_=cx.psA[:, :]))

        S = cx.sb("S", [64, 16 * 64], F32)
        Sb = cx.sb("Sb", [128, 16 * 64], BF16)
        eaTh = cx.sb("eaTh", [64, 16], F32)
        p.em("vector", lambda e: e.memset(S[:, :], 0.0))
        p.em("vector", lambda e: e.memset(Sb[:, :], 0.0))
        qf = cx.sb("qf", [128, 1024], F32)
        kf = cx.sb("kf", [128, 1024], F32)
        vb = cx.sb("vb", [128, 1024], BF16)
        gatef = cx.sb("gatef", [128, 1024], F32)
        gkt = cx.sb("gkt", [128, 1024], F32)
        acum = cx.sb("acum", [128, 1024], F32)
        arow = cx.sb("arow", [1, 1024], F32)
        eaT = cx.sb("eaT", [128, 16], F32)
        g1t = cx.sb("g1t", [16, 128], BF16)
        qbB = cx.sb("qbB", [128, 1024], BF16)
        kbB = cx.sb("kbB", [128, 1024], BF16)
        qTc = cx.sb("qTc", [128, NC * 128], BF16)
        kTc = cx.sb("kTc", [128, NC * 128], BF16)
        of = cx.sb("of", [128, 1024], F32)
        pb = cx.sb("pb", [128, 128], BF16)
        ogb = cx.sb("ogb", [128, 1024], BF16)
        ogT = cx.sb("ogT", [128, NC * 128], BF16)

        def proj(dst_ps, wslab, i, ch):
            for j in range(NC):
                p.em("tensor", lambda e, j=j, i=i, ch=ch, w=wslab, d=dst_ps: e.matmul(
                    d[:, :], zT[j][:, i * 128:(i + 1) * 128],
                    w[:, j * 1024 + ch * 512: j * 1024 + (ch + 1) * 512],
                    start=(j == 0), stop=(j == NC - 1)))

        for i in range(NT):
            for dst, wslab in ((qf, wq), (kf, wk), (gatef, wgt)):
                for ch in range(2):
                    proj(cx.psA, wslab, i, ch)
                    p.em("scalar", lambda e, ch=ch, dst=dst: e.copy(
                        out=dst[:, ch * 512:(ch + 1) * 512], in_=cx.psA[:, :]))
            for ch in range(2):
                proj(cx.psA, wv, i, ch)
                p.em("scalar", lambda e, ch=ch: e.copy(out=vb[:, ch * 512:(ch + 1) * 512],
                                                       in_=cx.psA[:, :]))
            for j in range(NC):
                p.em("tensor", lambda e, j=j, i=i: e.matmul(
                    cx.psB[:, 0:16], zT[j][:, i * 128:(i + 1) * 128], wg1[:, j * 16:(j + 1) * 16],
                    start=(j == 0), stop=(j == NC - 1)))
            p.em("scalar", lambda e: e.copy(out=cx.scA[:, 0:16], in_=cx.psB[:, 0:16]))
            p.em("vector", lambda e: e.tensor_copy(out=cx.bB[:, 0:16], in_=cx.scA[:, 0:16]))
            cx.tr(cx.bB[:, 0:16], g1t[0:16, 0:128], bf16=True, np_=128, nf=16)
            for ch in range(2):
                p.em("tensor", lambda e, ch=ch: e.matmul(
                    cx.psA[:, :], g1t[0:16, 0:128], wg2b[0:16, ch * 512:(ch + 1) * 512],
                    start=True, stop=True))
                p.em("vector", lambda e, ch=ch: e.tensor_add(
                    gkt[:, ch * 512:(ch + 1) * 512], bg2b[:, ch * 512:(ch + 1) * 512], cx.psA[:, :]))
            p.em("vector", lambda e: e.tensor_scalar_min(cx.scA[:, :], gkt[:, :], 0.0))
            p.em("scalar", lambda e: e.activation(cx.scB[:, :], gkt[:, :], AF.Abs))
            p.em("scalar", lambda e: e.activation(cx.scB[:, :], cx.scB[:, :], AF.Exp, scale=-1.0))
            p.em("scalar", lambda e: e.activation(cx.scB[:, :], cx.scB[:, :], AF.Ln, bias=1.0))
            p.em("vector", lambda e: e.tensor_sub(cx.scA[:, :], cx.scA[:, :], cx.scB[:, :]))
            p.em("vector", lambda e: e.tensor_scalar_mul(gkt[:, :], cx.scA[:, :], 1.0 / 16.0))
            for ch in range(2):
                p.em("tensor", lambda e, ch=ch: e.matmul(
                    cx.psA[:, :], cx.sTri[:, :], gkt[:, ch * 512:(ch + 1) * 512],
                    start=True, stop=True))
                p.em("scalar", lambda e, ch=ch: e.copy(out=acum[:, ch * 512:(ch + 1) * 512],
                                                       in_=cx.psA[:, :]))
            p.dma("sync", arow[:, :], acum[127:128, :])
            for j in range(NC):
                p.dma("sync", eaT[:, j:j + 1], arow[0:1, j * 128:(j + 1) * 128])
            p.em("scalar", lambda e: e.activation(eaT[:, 8:16], eaT[:, 0:8], AF.Exp))
            p.em("scalar", lambda e: e.activation(cx.scA[:, :], acum[:, :], AF.Exp))
            p.em("vector", lambda e: e.tensor_mul(qf[:, :], qf[:, :], cx.scA[:, :]))
            p.em("vector", lambda e: e.tensor_scalar_mul(qbB[:, :], qf[:, :], 0.125))
            p.em("scalar", lambda e: e.activation(cx.scB[:, :], acum[:, :], AF.Exp, scale=-1.0))
            p.em("vector", lambda e: e.tensor_mul(kf[:, :], kf[:, :], cx.scB[:, :]))
            p.em("vector", lambda e: e.tensor_copy(out=kbB[:, :], in_=kf[:, :]))
            for j in range(NC):
                cx.tr(qbB[:, j * 128:(j + 1) * 128], qTc[:, j * 128:(j + 1) * 128])
                cx.tr(kbB[:, j * 128:(j + 1) * 128], kTc[:, j * 128:(j + 1) * 128])
            for h in range(16):
                jt, off = h // 2, (h % 2) * 64
                p.em("tensor", lambda e, jt=jt, off=off: e.matmul(
                    cx.psA[:, 0:128], kTc[off:off + 64, jt * 128:(jt + 1) * 128],
                    qTc[off:off + 64, jt * 128:(jt + 1) * 128], start=True, stop=True))
                p.em("vector", lambda e: e.tensor_mul(pb[:, :], cx.psA[:, 0:128], cx.sTri[:, :]))
                p.em("tensor", lambda e, h=h: e.matmul(
                    cx.psB[:, 0:64], pb[:, :], vb[:, h * 64:(h + 1) * 64], start=True, stop=False))
                p.em("tensor", lambda e, h=h, jt=jt, off=off: e.matmul(
                    cx.psB[:, 0:64], qTc[off:off + 64, jt * 128:(jt + 1) * 128],
                    Sb[off:off + 64, h * 64:(h + 1) * 64], start=False, stop=True))
                p.em("scalar", lambda e, h=h: e.copy(out=of[:, h * 64:(h + 1) * 64],
                                                     in_=cx.psB[:, 0:64]))
                p.em("tensor", lambda e, h=h: e.matmul(
                    cx.psO[0:64, 0:64], kbB[:, h * 64:(h + 1) * 64], vb[:, h * 64:(h + 1) * 64],
                    start=True, stop=True))
                p.em("vector", lambda e, h=h: e.tensor_add(
                    cx.scB[0:64, 0:64], cx.psO[0:64, 0:64], S[0:64, h * 64:(h + 1) * 64]))
                p.em("vector", lambda e, h=h, jt=jt, off=off: e.tensor_copy(
                    out=eaTh[0:64, h:h + 1], in_=eaT[off:off + 64, 8 + jt:9 + jt]))
                p.em("vector", lambda e, h=h: e.tensor_scalar_mul(
                    S[0:64, h * 64:(h + 1) * 64], cx.scB[0:64, 0:64], eaTh[0:64, h:h + 1]))
                p.em("vector", lambda e, h=h: e.tensor_copy(out=Sb[0:64, h * 64:(h + 1) * 64],
                                                            in_=S[0:64, h * 64:(h + 1) * 64]))
                p.em("vector", lambda e, h=h: e.tensor_copy(out=Sb[64:128, h * 64:(h + 1) * 64],
                                                            in_=S[0:64, h * 64:(h + 1) * 64]))
            p.em("scalar", lambda e: e.activation(cx.scB[:, :], of[:, :], AF.Square))
            p.em("vector", lambda e: e.reduce_sum(cx.st[:, 8:24],
                                                  cx.scB[:, :].rearrange("p (h d) -> p h d", h=16),
                                                  axis=AX.X))
            p.em("vector", lambda e: e.tensor_scalar(out=cx.st[:, 8:24], in0=cx.st[:, 8:24],
                                                     scalar1=1.0 / 64, scalar2=1e-5,
                                                     op0=ALU.mult, op1=ALU.add))
            p.em("scalar", lambda e: e.sqrt(cx.st[:, 24:40], cx.st[:, 8:24]))
            p.em("vector", lambda e: e.reciprocal(cx.st[:, 40:56], cx.st[:, 24:40]))
            for h in range(16):
                p.em("vector", lambda e, h=h: e.tensor_scalar_mul(
                    of[:, h * 64:(h + 1) * 64], of[:, h * 64:(h + 1) * 64], cx.st[:, 40 + h:41 + h]))
            p.em("vector", lambda e: e.tensor_mul(of[:, :], of[:, :], gnwb[:, :]))
            p.em("scalar", lambda e: e.activation(cx.scB[:, :], gatef[:, :], AF.Sigmoid))
            p.em("vector", lambda e: e.tensor_mul(gatef[:, :], gatef[:, :], cx.scB[:, :]))
            p.em("vector", lambda e: e.tensor_mul(of[:, :], of[:, :], gatef[:, :]))
            p.em("vector", lambda e: e.tensor_copy(out=ogb[:, :], in_=of[:, :]))
            for j in range(NC):
                cx.tr(ogb[:, j * 128:(j + 1) * 128], ogT[:, j * 128:(j + 1) * 128])
            p.dma("sync", cx.xtile[:, :], xin[i * 128:(i + 1) * 128, :])
            for ch in range(2):
                for j in range(NC):
                    p.em("tensor", lambda e, j=j, ch=ch: e.matmul(
                        cx.psA[:, :], ogT[:, j * 128:(j + 1) * 128],
                        wo[:, j * 1024 + ch * 512: j * 1024 + (ch + 1) * 512],
                        start=(j == 0), stop=(j == NC - 1)))
                p.em("vector", lambda e, ch=ch: e.tensor_add(
                    cx.xtile[:, ch * 512:(ch + 1) * 512], cx.xtile[:, ch * 512:(ch + 1) * 512],
                    cx.psA[:, :]))
            p.dma("sync", xout[i * 128:(i + 1) * 128, :], cx.xtile[:, :])



def build_l1attn():
    nc = bass.Bass("TRN2", target_bir_lowering=False)
    p = Prog()
    es = ExitStack()
    with es:
        cx = Ctx(nc, p, es)
        cx.xin = cx.inp("xin", [T, D])
        cx.xout = nc.dram_tensor("xout", [T, D], F32, kind="ExternalOutput").ap()
        cx.setup_common()
        cx.cur = es.enter_context(ExitStack())
        emit_l1attn(cx)
        sem = es.enter_context(nc.semaphore("tick"))
        block = es.enter_context(nc.Block())
        p.finalize(nc, block, sem)
    return nc


def l1attn_inputs(full, x):
    c = host_consts()
    return {"xin": x,
            "Wq": full["L1_attn_Wq"], "Wk": full["L1_attn_Wk"], "Wv": full["L1_attn_Wv"],
            "Wg1": full["L1_attn_Wg1"], "Wg2": full["L1_attn_Wg2"],
            "bg2r": np.ascontiguousarray(full["L1_attn_bg2"][None, :]),
            "gnwr": np.ascontiguousarray(np.tile(full["L1_attn_g_norm_w"], 16)[None, :]),
            "Wg": full["L1_attn_Wg"], "Wo": full["L1_attn_Wo"],
            "identF": c["ident"], "m0tF": c["m0t"], "triF": c["tri01"]}

def _routing(cx, lg_hbm, iota8s, idxi, cB, tag):
    p = cx.p
    lg = cx.st[0:1, 16:24]
    rk = cx.st[0:1, 24:32]
    t8 = cx.st[0:1, 32:40]
    ex8 = cx.st[0:1, 40:48]
    cw = cx.st[0:1, 48:56]
    crow = cx.st[0:1, 56:58]
    p.dma("sync", lg, lg_hbm)
    for b in range(8):
        p.em("vector", lambda e, b=b: e.tensor_single_scalar(out=t8, in_=lg,
                                                             scalar=lg[0:1, b:b + 1], op=ALU.is_gt))
        p.em("vector", lambda e, b=b: e.reduce_sum(rk[0:1, b:b + 1], t8, axis=AX.X))
    p.em("vector", lambda e: e.reduce_max(cx.st[0:1, 58:59], lg, axis=AX.X))
    p.em("vector", lambda e: e.tensor_scalar_mul(cx.st[0:1, 58:59], cx.st[0:1, 58:59], -1.0))
    p.em("scalar", lambda e: e.activation(ex8, lg, AF.Exp, bias=cx.st[0:1, 58:59]))
    p.em("vector", lambda e: e.tensor_single_scalar(out=t8, in_=rk, scalar=1.5, op=ALU.is_lt))
    p.em("vector", lambda e: e.tensor_mul(ex8, ex8, t8))
    p.em("vector", lambda e: e.reduce_sum(cx.st[0:1, 59:60], ex8, axis=AX.X))
    p.em("vector", lambda e: e.reciprocal(cx.st[0:1, 60:61], cx.st[0:1, 59:60]))
    p.em("vector", lambda e: e.tensor_scalar_mul(cw, ex8, cx.st[0:1, 60:61]))
    for s in range(2):
        p.em("vector", lambda e, s=s: e.tensor_single_scalar(out=t8, in_=rk,
                                                             scalar=float(s) + 0.5 if False else float(s),
                                                             op=ALU.is_equal))
        p.em("vector", lambda e: e.tensor_mul(ex8, t8, iota8s[0:1, 0:8]))
        p.em("vector", lambda e, s=s: e.reduce_sum(cx.st[0:1, 61 + s:62 + s], ex8, axis=AX.X))
        p.em("vector", lambda e, s=s: e.tensor_copy(out=idxi[0:1, s:s + 1],
                                                    in_=cx.st[0:1, 61 + s:62 + s]))
        p.em("vector", lambda e: e.tensor_mul(t8, t8, cw))
        p.em("vector", lambda e, s=s: e.reduce_sum(crow[0:1, s:s + 1], t8, axis=AX.X))
    cx.bcast_row(crow, cB[:, 0:2], 128, 2)


def emit_l0ffn(cx):
    nc, p, es = cx.nc, cx.p, cx.cur
    if True:
        xin = cx.xin
        wqT = cx.inp("wqT", [D, D])
        wkT = cx.inp("wkT", [D, 256])
        wvT = cx.inp("wvT", [D, 256])
        woT = cx.inp("woT", [D, D])
        qg16r = cx.inp("qg16r", [1, 16])
        kg16r = cx.inp("kg16r", [1, 16])
        kg4r = cx.inp("kg4r", [1, 4])
        fc_s1 = cx.inp("fc_s1", [8, 1024, 64])
        fc_s2 = cx.inp("fc_s2", [8, 4096, 64])
        fcmixT2 = cx.inp("fcmixT2", [8, 2048, 4096])
        fc_lg = cx.inp("fc_lg", [1, 8])
        pj_s1 = cx.inp("pj_s1", [8, 4096, 64])
        pj_s2 = cx.inp("pj_s2", [8, 1024, 64])
        pjmixT2 = cx.inp("pjmixT2", [8, 8192, 1024])
        pj_lg = cx.inp("pj_lg", [1, 8])
        r0c = cx.inp("r0c", [T, D])
        r0s = cx.inp("r0s", [T, D])
        iota8f = cx.inp("iota8f", [1, 8])
        xout = cx.xout
        yscr = cx.internal("yscr", [T, D])
        fcscr = cx.internal("fcscr", [2, 1024, 4096])
        pjscr = cx.internal("pjscr", [2, 4096, 1024])
        s1fscr = cx.internal("s1fscr", [2, 1024, 64])
        s2fscr = cx.internal("s2fscr", [2, 4096, 64])
        s1pscr = cx.internal("s1pscr", [2, 4096, 64])
        s2pscr = cx.internal("s2pscr", [2, 1024, 64])
        iota8s = cx.sb("iota8s", [1, 8], F32)
        p.dma("sync", iota8s[:, :], iota8f)

        hT = [cx.sb(f"hT{j}", [128, T], BF16) for j in range(NC)]

        esA = es.enter_context(ExitStack())
        sbA = cx.mk(esA)
        ytile = sbA("ytile", [128, 1024], F32)
        qwork = sbA("qwork", [128, 1024], F32)
        rwork = sbA("rwork", [128, 1024], F32)
        ctile = sbA("ctile", [128, 1024], F32)
        stile = sbA("stile", [128, 1024], F32)
        qT = [sbA(f"qT{j}", [128, T], BF16) for j in range(NC)]
        kT2 = [sbA(f"kT2{j}", [128, T], BF16) for j in range(2)]
        kTs = [sbA(f"kTs{j}", [128, T], BF16) for j in range(2)]
        vext4 = sbA("vext4", [128, NT * 4 * 65], BF16)
        qgb = sbA("qgb", [128, 16], F32)
        kgb4 = sbA("kgb4", [128, 4], F32)
        mneg = sbA("mneg", [128, 16], F32)
        grow = sbA("grow", [1, 16], F32)
        g4 = sbA("g4", [1, 4], F32)
        p.dma("sync", grow[:, :], qg16r)
        cx.bcast_row(grow[0:1, 0:16], qgb[:, :], 128, 16)
        p.dma("sync", g4[:, :], kg4r)
        cx.bcast_row(g4[0:1, 0:4], kgb4[:, :], 128, 4)
        p.dma("sync", cx.st[0:1, 16:32], kg16r)
        p.em("vector", lambda e: e.tensor_mul(grow[0:1, 0:16], grow[0:1, 0:16], cx.st[0:1, 16:32]))
        p.em("vector", lambda e: e.tensor_scalar_mul(grow[0:1, 0:16], grow[0:1, 0:16], -8.0))
        cx.bcast_row(grow[0:1, 0:16], mneg[:, :], 128, 16)

        esA1 = esA.enter_context(ExitStack())
        sbA1 = cx.mk(esA1)
        zT = [sbA1(f"zT{j}", [128, T], BF16) for j in range(NC)]
        wslab = sbA1("wslab", [128, NC * 1024], BF16)
        for i in range(NT):
            p.dma("sync", cx.xtile[:, :], xin[i * 128:(i + 1) * 128, :])
            cx.norm_tile(cx.xtile[:, :], ytile[:, :], None, False, 1e-5)
            p.dma("sync", yscr[i * 128:(i + 1) * 128, :], ytile[:, :])
            cx.norm_tile(ytile[:, :], None, cx.bB[:, :], True, 1e-6)
            for j in range(NC):
                cx.tr(cx.bB[:, j * 128:(j + 1) * 128], zT[j][:, i * 128:(i + 1) * 128])

        def headnorm_rope(buf, nh, gb, eps):
            w = nh * 64
            p.em("scalar", lambda e: e.activation(cx.scB[:, 0:w], buf[:, 0:w], AF.Square))
            p.em("vector", lambda e: e.reduce_sum(cx.st[:, 8:8 + nh],
                                                  buf[:, 0:w].rearrange("p (h d) -> p h d", d=64) if False else
                                                  cx.scB[:, 0:w].rearrange("p (h d) -> p h d", d=64),
                                                  axis=AX.X))
            p.em("vector", lambda e: e.tensor_scalar(out=cx.st[:, 8:8 + nh], in0=cx.st[:, 8:8 + nh],
                                                     scalar1=1.0 / 64, scalar2=eps,
                                                     op0=ALU.mult, op1=ALU.add))
            p.em("scalar", lambda e: e.sqrt(cx.st[:, 24:24 + nh], cx.st[:, 8:8 + nh]))
            p.em("vector", lambda e: e.reciprocal(cx.st[:, 40:40 + nh], cx.st[:, 24:24 + nh]))
            p.em("vector", lambda e, gb=gb: e.tensor_mul(cx.st[:, 40:40 + nh], cx.st[:, 40:40 + nh],
                                                         gb[:, 0:nh]))
            for h in range(nh):
                p.em("vector", lambda e, h=h: e.tensor_scalar_mul(
                    buf[:, h * 64:(h + 1) * 64], buf[:, h * 64:(h + 1) * 64], cx.st[:, 40 + h:41 + h]))
            bv = buf[:, 0:w].rearrange("p (h two f) -> p h two f", two=2, f=32)
            rv = rwork[:, 0:w].rearrange("p (h two f) -> p h two f", two=2, f=32)
            p.em("vector", lambda e: e.tensor_copy(out=rv[:, :, 0, :], in_=bv[:, :, 1, :]))
            p.em("vector", lambda e: e.tensor_copy(out=rv[:, :, 1, :], in_=bv[:, :, 0, :]))
            p.em("vector", lambda e, w=w: e.tensor_mul(buf[:, 0:w], buf[:, 0:w], ctile[:, 0:w]))
            p.em("vector", lambda e, w=w: e.tensor_mul(rwork[:, 0:w], rwork[:, 0:w], stile[:, 0:w]))
            p.em("vector", lambda e, w=w: e.tensor_add(buf[:, 0:w], buf[:, 0:w], rwork[:, 0:w]))

        load_w_bf16(cx, wqT, wslab, D, 1024)
        for i in range(NT):
            p.dma("sync", ctile[:, :], r0c[i * 128:(i + 1) * 128, :])
            p.dma("sync", stile[:, :], r0s[i * 128:(i + 1) * 128, :])
            for ch in range(2):
                for j in range(NC):
                    p.em("tensor", lambda e, i=i, j=j, ch=ch: e.matmul(
                        cx.psA[:, :], zT[j][:, i * 128:(i + 1) * 128],
                        wslab[:, j * 1024 + ch * 512: j * 1024 + (ch + 1) * 512],
                        start=(j == 0), stop=(j == NC - 1)))
                p.em("scalar", lambda e, ch=ch: e.copy(out=qwork[:, ch * 512:(ch + 1) * 512],
                                                       in_=cx.psA[:, :]))
            headnorm_rope(qwork, 16, qgb, 1e-6)
            p.em("vector", lambda e: e.tensor_copy(out=cx.bB[:, :], in_=qwork[:, :]))
            for j in range(NC):
                cx.tr(cx.bB[:, j * 128:(j + 1) * 128], qT[j][:, i * 128:(i + 1) * 128])
        load_w_bf16(cx, wkT, wslab, D, 256)
        for i in range(NT):
            p.dma("sync", ctile[:, 0:256], r0c[i * 128:(i + 1) * 128, 0:256])
            p.dma("sync", stile[:, 0:256], r0s[i * 128:(i + 1) * 128, 0:256])
            for j in range(NC):
                p.em("tensor", lambda e, i=i, j=j: e.matmul(
                    cx.psA[:, 0:256], zT[j][:, i * 128:(i + 1) * 128],
                    wslab[:, j * 256:(j + 1) * 256], start=(j == 0), stop=(j == NC - 1)))
            p.em("scalar", lambda e: e.copy(out=qwork[:, 0:256], in_=cx.psA[:, 0:256]))
            headnorm_rope(qwork, 4, kgb4, 1e-6)
            p.em("vector", lambda e: e.tensor_copy(out=cx.bB[:, 0:256], in_=qwork[:, 0:256]))
            for j in range(2):
                cx.tr(cx.bB[:, j * 128:(j + 1) * 128], kT2[j][:, i * 128:(i + 1) * 128])
        for j in range(2):
            p.em("vector", lambda e, j=j: e.tensor_copy(out=kTs[j][0:64, :], in_=kT2[j][64:128, :]))
            p.em("vector", lambda e, j=j: e.tensor_copy(out=kTs[j][64:128, :], in_=kT2[j][0:64, :]))
        load_w_bf16(cx, wvT, wslab, D, 256)
        for i in range(NT):
            for j in range(NC):
                p.em("tensor", lambda e, i=i, j=j: e.matmul(
                    cx.psA[:, 0:256], zT[j][:, i * 128:(i + 1) * 128],
                    wslab[:, j * 256:(j + 1) * 256], start=(j == 0), stop=(j == NC - 1)))
            dst = vext4[:, i * 4 * 65:(i + 1) * 4 * 65].rearrange("p (h w) -> p h w", w=65)
            p.em("vector", lambda e, dst=dst: e.tensor_copy(
                out=dst[:, :, 0:64], in_=cx.psA[:, 0:256].rearrange("p (h d) -> p h d", d=64)))
            p.em("vector", lambda e, dst=dst: e.memset(dst[:, :, 64], 1.0))
        esA1.close()

        attT = [sbA(f"attT{j}", [128, T], BF16) for j in range(NC)]
        pexp = sbA("pexp", [128, NT * 512], BF16)
        rrow = sbA("rrow", [1, 512], F32)
        rb = sbA("rb", [64, 512], F32)
        wslab2 = sbA("wslab2", [128, NC * 1024], BF16)
        for h in range(16):
            jt, off = h // 2, (h % 2) * 64
            kv = h // 4
            kvt, kvo = kv // 2, (kv % 2) * 64
            ksl = kT2[kvt] if kvo == off else kTs[kvt]
            for c in range(4):
                ntk = 4 * c + 4
                for kt in range(ntk):
                    p.em("tensor", lambda e, kt=kt, c=c, ksl=ksl, off=off, jt=jt: e.matmul(
                        cx.psA[:, :], ksl[off:off + 64, kt * 128:(kt + 1) * 128],
                        qT[jt][off:off + 64, c * 512:(c + 1) * 512], start=True, stop=True))
                    q4d = kt - 4 * c
                    if q4d < 0:
                        p.em("scalar", lambda e, kt=kt, h=h: e.activation(
                            pexp[:, kt * 512:(kt + 1) * 512], cx.psA[:, :], AF.Exp,
                            bias=mneg[:, h:h + 1], scale=0.125))
                    else:
                        if q4d > 0:
                            p.em("vector", lambda e, kt=kt, q4d=q4d: e.memset(
                                pexp[:, kt * 512: kt * 512 + q4d * 128], 0.0))
                        p.em("vector", lambda e, q4d=q4d: e.tensor_add(
                            cx.scB[:, 0:128], cx.psA[:, q4d * 128:(q4d + 1) * 128], cx.sM0[:, :]))
                        p.em("scalar", lambda e, kt=kt, q4d=q4d, h=h: e.activation(
                            pexp[:, kt * 512 + q4d * 128: kt * 512 + (q4d + 1) * 128],
                            cx.scB[:, 0:128], AF.Exp, bias=mneg[:, h:h + 1], scale=0.125))
                        if q4d < 3:
                            p.em("scalar", lambda e, kt=kt, q4d=q4d, h=h: e.activation(
                                pexp[:, kt * 512 + (q4d + 1) * 128:(kt + 1) * 512],
                                cx.psA[:, (q4d + 1) * 128:], AF.Exp,
                                bias=mneg[:, h:h + 1], scale=0.125))
                for kt in range(ntk):
                    p.em("tensor", lambda e, kt=kt, kv=kv, ntk=ntk: e.matmul(
                        cx.psO[0:65, :], vext4[:, kt * 4 * 65 + kv * 65: kt * 4 * 65 + (kv + 1) * 65],
                        pexp[:, kt * 512:(kt + 1) * 512], start=(kt == 0), stop=(kt == ntk - 1)))
                p.em("vector", lambda e: e.reciprocal(rrow[0:1, :], cx.psO[64:65, :]))
                p.em("tensor", lambda e: e.matmul(cx.psB[0:64, :], cx.sOnes[0:1, 0:64], rrow[0:1, :],
                                                  start=True, stop=True))
                p.em("scalar", lambda e: e.copy(out=rb[:, :], in_=cx.psB[0:64, :]))
                p.em("vector", lambda e, c=c, jt=jt, off=off: e.tensor_mul(
                    attT[jt][off:off + 64, c * 512:(c + 1) * 512], cx.psO[0:64, :], rb[:, :]))
        load_w_bf16(cx, woT, wslab2, D, 1024)
        for i in range(NT):
            p.dma("sync", ytile[:, :], yscr[i * 128:(i + 1) * 128, :])
            for ch in range(2):
                for j in range(NC):
                    p.em("tensor", lambda e, i=i, j=j, ch=ch: e.matmul(
                        cx.psA[:, :], attT[j][:, i * 128:(i + 1) * 128],
                        wslab2[:, j * 1024 + ch * 512: j * 1024 + (ch + 1) * 512],
                        start=(j == 0), stop=(j == NC - 1)))
                p.em("vector", lambda e, ch=ch: e.tensor_add(
                    ytile[:, ch * 512:(ch + 1) * 512], ytile[:, ch * 512:(ch + 1) * 512], cx.psA[:, :]))
            p.dma("sync", yscr[i * 128:(i + 1) * 128, :], ytile[:, :])
            cx.norm_tile(ytile[:, :], None, cx.bB[:, :], True, 1e-6)
            for j in range(NC):
                cx.tr(cx.bB[:, j * 128:(j + 1) * 128], hT[j][:, i * 128:(i + 1) * 128])
        esA.close()

        esB = es.enter_context(ExitStack())
        sbB = cx.mk(esB)
        idfc = sbB("idfc", [1, 2], I32)
        idpj = sbB("idpj", [1, 2], I32)
        cfcB = sbB("cfcB", [128, 2], F32)
        cpjB = sbB("cpjB", [128, 2], F32)
        _routing(cx, fc_lg, iota8s, idfc, cfcB, "fc")
        _routing(cx, pj_lg, iota8s, idpj, cpjB, "pj")
        if not hasattr(cx, "ffn_regs"):
            cx.ffn_regs = REGS = {}
            def loadregs(e):
                for nm, ap in (("fc0", idfc[0:1, 0:1]), ("fc1", idfc[0:1, 1:2]),
                               ("pj0", idpj[0:1, 0:1]), ("pj1", idpj[0:1, 1:2])):
                    r = e.value_load(ap)
                    REGS[nm] = r
                return e.dma_start(out=cx.st[0:1, 62:63], in_=cx.st[0:1, 63:64])
            p.em("sync", loadregs, inc=16)
            fcm2 = fcmixT2.rearrange("b r c -> (b r) c")
            pjm2 = pjmixT2.rearrange("b r c -> (b r) c")
            for ss in range(2):
                p.em("sync", lambda e, ss=ss: e.dma_start(
                    out=fcscr[ss], in_=fcm2[bass.ds(REGS[f"fc{ss}"] * (2048 + 64), 1024), :]), inc=16)
                p.em("sync", lambda e, ss=ss: e.dma_start(
                    out=pjscr[ss], in_=pjm2[bass.ds(REGS[f"pj{ss}"] * (8192 + 64), 4096), :]), inc=16)
                p.em("sync", lambda e, ss=ss: e.dma_start(
                    out=s1fscr[ss], in_=fc_s1[bass.ds(REGS[f"fc{ss}"], 1)].squeeze(0)), inc=16)
                p.em("sync", lambda e, ss=ss: e.dma_start(
                    out=s2fscr[ss], in_=fc_s2[bass.ds(REGS[f"fc{ss}"], 1)].squeeze(0)), inc=16)
                p.em("sync", lambda e, ss=ss: e.dma_start(
                    out=s1pscr[ss], in_=pj_s1[bass.ds(REGS[f"pj{ss}"], 1)].squeeze(0)), inc=16)
                p.em("sync", lambda e, ss=ss: e.dma_start(
                    out=s2pscr[ss], in_=pj_s2[bass.ds(REGS[f"pj{ss}"], 1)].squeeze(0)), inc=16)
        REGS = cx.ffn_regs
        s1f = [sbB(f"s1f{s}", [128, 1024], BF16) for s in range(2)]
        s2f = [sbB(f"s2f{s}", [128, 4096], BF16) for s in range(2)]
        s1p = [sbB(f"s1p{s}", [128, 4096], BF16) for s in range(2)]
        s2p = [sbB(f"s2p{s}", [128, 1024], BF16) for s in range(2)]

        def gather_sel(rkey, src, dst, ncol, scaleB=None):
            for q in range(ncol // 1024):
                src3 = src.rearrange("(g c) d -> c g d", c=64)
                def gg(e, q=q, src3=src3):
                    return e.dma_start(out=cx.scA[0:64, 0:1024],
                                       in_=src3[:, q * 16:(q + 1) * 16, :])
                p.em("sync", gg, inc=16)
                for base in (0, 64):
                    if scaleB is None:
                        p.em("vector", lambda e, q=q, base=base, dst=dst: e.tensor_copy(
                            out=dst[base:base + 64, q * 1024:(q + 1) * 1024], in_=cx.scA[0:64, 0:1024]))
                    else:
                        p.em("vector", lambda e, q=q, base=base, dst=dst, scaleB=scaleB: e.tensor_scalar_mul(
                            dst[base:base + 64, q * 1024:(q + 1) * 1024], cx.scA[0:64, 0:1024], scaleB))
        for s in range(2):
            gather_sel("", s1fscr[s], s1f[s], 1024)
            gather_sel("", s2fscr[s], s2f[s], 4096, cfcB[0:64, s:s + 1])
            gather_sel("", s1pscr[s], s1p[s], 4096)
            gather_sel("", s2pscr[s], s2p[s], 1024, cpjB[0:64, s:s + 1])

        u1all = sbB("u1all", [128, 32 * 512], BF16)
        u1fc = [u1all[:, ss * 4096:(ss + 1) * 4096] for ss in range(2)]
        u1pj = u1all[:, 0:16384]
        h2T = sbB("h2T", [128, 32 * 512], BF16)
        mslab = sbB("mslab", [128, 32 * 128], BF16)
        mxf = mslab[:, 0:1024]
        mxf2 = mslab[:, 1024:2048]
        mxp = mslab[:, 0:4096]
        m0b = sbB("m0b", [128, 512], BF16)
        m1b = sbB("m1b", [128, 512], BF16)
        mpj = sbB("mpj", [128, 512], BF16)
        otile = [sbB(f"otile{t}", [128, 1024], F32) for t in range(4)]
        fcscr3 = [fcscr[ss].rearrange("(jt p) c -> p jt c", p=128) for ss in range(2)]
        pjscr3 = [pjscr[ss].rearrange("(jt p) c -> p jt c", p=128) for ss in range(2)]

        for tc in range(4):
            for ss in range(2):
                for g in range(16):
                    gb = (g % 2) * 64
                    p.em("tensor", lambda e, ss=ss, g=g, gb=gb, tc=tc: e.matmul(
                        cx.psO[0:64, :], s1f[ss][gb:gb + 64, g * 64:(g + 1) * 64],
                        hT[g // 2][gb:gb + 64, tc * 512:(tc + 1) * 512],
                        start=True, stop=True))
                    p.em("scalar", lambda e, ss=ss, g=g, gb=gb: e.copy(
                        out=u1fc[ss][gb:gb + 64, (g // 2) * 512:(g // 2 + 1) * 512],
                        in_=cx.psO[0:64, :]))
            for ot in range(32):
                for ss, mx in ((0, mxf), (1, mxf2)):
                    p.dma("sync", cx.scA[:, :],
                          fcscr3[ss][:, :, ot * 128:(ot + 1) * 128])
                    p.em("vector", lambda e, mx=mx: e.tensor_copy(out=mx, in_=cx.scA[:, :]))
                for ss, mx, psp in ((0, mxf, cx.psA), (1, mxf2, cx.psB)):
                    for jt in range(NC):
                        p.em("tensor", lambda e, ss=ss, jt=jt, mx=mx, psp=psp: e.matmul(
                            psp[:, :], mx[:, jt * 128:(jt + 1) * 128],
                            u1fc[ss][:, jt * 512:(jt + 1) * 512],
                            start=(jt == 0), stop=(jt == NC - 1)))
                p.em("scalar", lambda e: e.copy(out=m0b[:, :], in_=cx.psA[:, :]))
                p.em("scalar", lambda e: e.copy(out=m1b[:, :], in_=cx.psB[:, :]))
                for gloc in range(2):
                    g2i = 2 * ot + gloc
                    for si, (s2s, mb) in enumerate(((s2f[0], m0b), (s2f[1], m1b))):
                        p.em("tensor", lambda e, gloc=gloc, g2i=g2i, s2s=s2s, mb=mb, si=si: e.matmul(
                            cx.psO[gloc * 64:(gloc + 1) * 64, :],
                            s2s[gloc * 64:(gloc + 1) * 64, g2i * 64:(g2i + 1) * 64],
                            mb[gloc * 64:(gloc + 1) * 64, :],
                            start=(si == 0), stop=(si == 1)))
                for gloc in range(2):
                    p.em("scalar", lambda e, gloc=gloc: e.activation(
                        cx.scA[0:64, 0:512], cx.psO[gloc * 64:(gloc + 1) * 64, :], AF.Relu))
                    p.em("vector", lambda e, gloc=gloc, ot=ot: e.tensor_tensor(
                        out=h2T[gloc * 64:(gloc + 1) * 64, ot * 512:(ot + 1) * 512],
                        in0=cx.scA[0:64, 0:512], in1=cx.scA[0:64, 0:512], op=ALU.mult))
            for t4 in range(4):
                gi = tc * 4 + t4
                p.dma("sync", otile[t4][:, :], yscr[gi * 128:(gi + 1) * 128, :])
            for ss in range(2):
                for g in range(64):
                    gb = (g % 2) * 64
                    p.em("tensor", lambda e, ss=ss, g=g, gb=gb: e.matmul(
                        cx.psO[0:64, :], s1p[ss][gb:gb + 64, g * 64:(g + 1) * 64],
                        h2T[gb:gb + 64, (g // 2) * 512:(g // 2 + 1) * 512],
                        start=True, stop=True))
                    p.em("scalar", lambda e, g=g, gb=gb: e.copy(
                        out=u1pj[gb:gb + 64, (g // 2) * 512:(g // 2 + 1) * 512],
                        in_=cx.psO[0:64, :]))
                for ot in range(8):
                    for q in range(4):
                        p.dma("sync", cx.scA[:, :],
                              pjscr3[ss][:, q * 8:(q + 1) * 8, ot * 128:(ot + 1) * 128])
                        p.em("vector", lambda e, q=q: e.tensor_copy(
                            out=mxp[:, q * 1024:(q + 1) * 1024], in_=cx.scA[:, :]))
                    for jt in range(32):
                        p.em("tensor", lambda e, jt=jt: e.matmul(
                            cx.psA[:, :], mxp[:, jt * 128:(jt + 1) * 128],
                            u1pj[:, jt * 512:(jt + 1) * 512],
                            start=(jt == 0), stop=(jt == 31)))
                    p.em("scalar", lambda e: e.copy(out=mpj[:, :], in_=cx.psA[:, :]))
                    for t4 in range(4):
                        for gloc in range(2):
                            g2i = 2 * ot + gloc
                            p.em("tensor", lambda e, t4=t4, gloc=gloc, g2i=g2i, ss=ss: e.matmul(
                                cx.psB[:, 0:64], mpj[gloc * 64:(gloc + 1) * 64, t4 * 128:(t4 + 1) * 128],
                                s2p[ss][gloc * 64:(gloc + 1) * 64, g2i * 64:(g2i + 1) * 64],
                                start=True, stop=True))
                            p.em("vector", lambda e, t4=t4, g2i=g2i: e.tensor_add(
                                otile[t4][:, g2i * 64:(g2i + 1) * 64],
                                otile[t4][:, g2i * 64:(g2i + 1) * 64], cx.psB[:, 0:64]))
            for t4 in range(4):
                gi = tc * 4 + t4
                p.dma("sync", cx.xtile[:, :], xin[gi * 128:(gi + 1) * 128, :])
                p.em("vector", lambda e, t4=t4: e.scalar_tensor_tensor(
                    out=cx.xtile[:, :], in0=otile[t4][:, :], scalar=0.5, in1=cx.xtile[:, :],
                    op0=ALU.mult, op1=ALU.add))
                p.dma("sync", xout[gi * 128:(gi + 1) * 128, :], cx.xtile[:, :])
        esB.close()



def build_l0ffn():
    nc = bass.Bass("TRN2", target_bir_lowering=False)
    p = Prog()
    es = ExitStack()
    with es:
        cx = Ctx(nc, p, es)
        cx.xin = cx.inp("xin", [T, D])
        cx.xout = nc.dram_tensor("xout", [T, D], F32, kind="ExternalOutput").ap()
        cx.setup_common()
        cx.cur = es.enter_context(ExitStack())
        emit_l0ffn(cx)
        sem = es.enter_context(nc.semaphore("tick"))
        block = es.enter_context(nc.Block())
        p.finalize(nc, block, sem)
    return nc


def _mixT2(mix, nin):
    B, o, j = mix.shape
    blk = 64
    bm = mix.reshape(B, o, j // (blk * nin) if False else nin, blk) if False else None
    m = mix.reshape(B, o, blk, nin)
    bm = np.ascontiguousarray(m.transpose(0, 3, 2, 1)).reshape(B, nin * blk, o)
    return np.ascontiguousarray(np.concatenate([bm, bm], axis=1))


def l0ffn_inputs(full, x):
    c = host_consts()
    return {"xin": x,
            "wqT": np.ascontiguousarray(full["L0_ffn_wq"].T),
            "wkT": np.ascontiguousarray(full["L0_ffn_wk"].T),
            "wvT": np.ascontiguousarray(full["L0_ffn_wv"].T),
            "woT": np.ascontiguousarray(full["L0_ffn_wo"].T),
            "qg16r": np.ascontiguousarray(full["L0_ffn_q_gain"][None, :]),
            "kg16r": np.ascontiguousarray(np.repeat(full["L0_ffn_k_gain"], 4)[None, :]),
            "kg4r": np.ascontiguousarray(full["L0_ffn_k_gain"][None, :]),
            "fc_s1": full["L0_ffn_fc_s1"], "fc_s2": full["L0_ffn_fc_s2"],
            "fcmixT2": _mixT2(full["L0_ffn_fc_mix"], 16),
            "fc_lg": np.ascontiguousarray(full["L0_ffn_fc_logits"][None, :]),
            "pj_s1": full["L0_ffn_pj_s1"], "pj_s2": full["L0_ffn_pj_s2"],
            "pjmixT2": _mixT2(full["L0_ffn_pj_mix"], 64),
            "pj_lg": np.ascontiguousarray(full["L0_ffn_pj_logits"][None, :]),
            "r0c": c["r0c"], "r0s": c["r0s"],
            "iota8f": np.arange(8, dtype=np.float32)[None, :],
            "identF": c["ident"], "m0tF": c["m0t"], "triF": c["tri01"]}

_PROGS = {}


def _get_prog(name):
    if name not in _PROGS:
        _PROGS[name] = {"l0ffn": build_l0ffn, "l0attn": build_l0attn,
                        "l1ffn": build_l1ffn, "l1attn": build_l1attn}[name]()
    return _PROGS[name]


def kernel(**inputs):
    if FUSED:
        return kernel_fused(inputs)
    full = inputs
    xs = [np.ascontiguousarray(inputs["L0_attn_x"][i]) for i in range(8)]
    seq = [("l0ffn", l0ffn_inputs), ("l0attn", l0attn_inputs), ("l0ffn", l0ffn_inputs),
           ("l1ffn", l1ffn_inputs), ("l1attn", l1attn_inputs), ("l1ffn", l1ffn_inputs)]
    wmaps = {}
    for name, fn in seq:
        if name not in wmaps:
            m = fn(full, xs[0])
            del m["xin"]
            wmaps[name] = m
    for name, _fn in seq:
        nc = _get_prog(name)
        in_maps = [dict(wmaps[name], xin=xs[i]) for i in range(8)]
        res = run_bass_kernel_spmd(nc, in_maps, core_ids=list(range(8)))
        xs = [np.asarray(res.results[i]["xout"]) for i in range(8)]
    return np.stack(xs, axis=0)

def build_fused():
    nc = bass.Bass("TRN2", target_bir_lowering=False)
    p = Prog()
    es = ExitStack()
    with es:
        cx = Ctx(nc, p, es)
        xin = nc.dram_tensor("xin", [T, D], F32, kind="ExternalInput").ap()
        xout = nc.dram_tensor("xout", [T, D], F32, kind="ExternalOutput").ap()
        xA = nc.dram_tensor("xA", [T, D], F32, kind="Internal").ap()
        xB = nc.dram_tensor("xB", [T, D], F32, kind="Internal").ap()
        cx.setup_common()
        stages = [(emit_l0ffn, "F_", xin, xA), (emit_l0attn, "A_", xA, xB),
                  (emit_l0ffn, "F_", xB, xA), (emit_l1ffn, "M_", xA, xB),
                  (emit_l1attn, "G_", xB, xA), (emit_l1ffn, "M_", xA, xout)]
        for fn, pref, xi, xo in stages:
            cx.prefix = pref
            cx.xin, cx.xout = xi, xo
            cx.cur = ExitStack()
            with cx.cur:
                fn(cx)
        cx.cur = es
        sem = es.enter_context(nc.semaphore("tick"))
        block = es.enter_context(nc.Block())
        p.finalize(nc, block, sem)
    return nc


FUSED = True


def _prefixed(fn, full, pref):
    m = fn(full, None)
    for k in ("xin", "identF", "m0tF", "triF"):
        m.pop(k, None)
    return {pref + k: v for k, v in m.items()}


def kernel_fused(inputs):
    xfull = inputs["L0_attn_x"]
    nc = _PROGS.get("fused")
    if nc is None:
        nc = _PROGS["fused"] = build_fused()
    c = host_consts()
    wmap = {"identF": c["ident"], "m0tF": c["m0t"], "triF": c["tri01"]}
    wmap.update(_prefixed(l0ffn_inputs, inputs, "F_"))
    wmap.update(_prefixed(l0attn_inputs, inputs, "A_"))
    wmap.update(_prefixed(l1ffn_inputs, inputs, "M_"))
    wmap.update(_prefixed(l1attn_inputs, inputs, "G_"))
    in_maps = [dict(wmap, xin=np.ascontiguousarray(xfull[i])) for i in range(8)]
    res = run_bass_kernel_spmd(nc, in_maps, core_ids=list(range(8)))
    return np.stack([np.asarray(res.results[i]["xout"]) for i in range(8)], axis=0)
